# version 17
# baseline (speedup 1.0000x reference)
import math
import numpy as np
import ml_dtypes
import concourse.bass as bass
import concourse.mybir as mybir
from concourse.bass_utils import run_bass_kernel_spmd

F32 = mybir.dt.float32
BF16 = mybir.dt.bfloat16
AF = mybir.ActivationFunctionType
ALU = mybir.AluOpType
AX = mybir.AxisListType

MAGIC = 12582912.0
NEG = -1e30
BF = ml_dtypes.bfloat16

B, T, D = 4, 4096, 1024
NLT = 16
H = 16
NEXP = 10
ENGS = ("sync", "gpsimd", "vector", "scalar", "tensor")
SBUF_BASE = 16512
SBUF_LIMIT = 229344



def _ktile(wT):
    K, N = wT.shape
    assert K % 128 == 0
    return np.ascontiguousarray(
        wT.reshape(K // 128, 128, N).transpose(1, 0, 2).reshape(128, -1))


def _sinusoidal(n, d=64):
    pos = np.arange(n, dtype=np.float32)[:, None]
    div = np.exp(np.arange(0, d, 2, dtype=np.float32) * (-math.log(10000.0) / d))
    ang = pos * div
    return np.stack([np.sin(ang), np.cos(ang)], axis=-1).reshape(n, d).astype(np.float32)


_S316 = _sinusoidal(316, 64)


def _rot_table(positions):
    S = _S316[positions]
    n = S.shape[0]
    M = np.empty((n, 64), np.float32)
    M[:, 0::2] = -S[:, 32:]
    M[:, 1::2] = S[:, :32]
    return np.tile(M, (1, H)).astype(BF)


def _host_prep(inp):
    g = {}
    wat = np.ascontiguousarray(np.asarray(inp["L0_attn_w_attn"], np.float32).T)
    g["wqT"] = _ktile(wat[:, :1024])
    g["wkvT"] = _ktile(wat[:, 1024:])
    g["bq"] = np.asarray(inp["L0_attn_b_attn"], np.float32)[None, :1024].copy()
    g["bkv"] = np.asarray(inp["L0_attn_b_attn"], np.float32)[None, 1024:].copy()
    g["wprojT"] = _ktile(np.ascontiguousarray(np.asarray(inp["L0_attn_w_proj"], np.float32).T))
    g["bproj"] = np.asarray(inp["L0_attn_b_proj"], np.float32)[None, :].copy()
    g["wassignT"] = _ktile(np.ascontiguousarray(np.asarray(inp["L0_attn_w_assign"], np.float32).T))
    g["wgF"] = _ktile(np.ascontiguousarray(np.asarray(inp["L0_ffn_w_gate"], np.float32).T))
    g["wuF"] = _ktile(np.ascontiguousarray(np.asarray(inp["L0_ffn_w_up"], np.float32).T))
    g["wdF"] = _ktile(np.ascontiguousarray(np.asarray(inp["L0_ffn_w_down"], np.float32).T))
    winT = np.ascontiguousarray(np.asarray(inp["L1_attn_W_in"], np.float32).T)
    g["winT_sl"] = np.ascontiguousarray(
        _ktile(winT).reshape(128, 8, 24, 128).transpose(2, 0, 1, 3).reshape(24, 128, 1024))
    dtwT = np.ascontiguousarray(np.asarray(inp["L1_attn_dt_w"], np.float32).T)
    g["dtwT_sl"] = np.ascontiguousarray(
        _ktile(dtwT).reshape(128, 12, 12, 128).transpose(2, 0, 1, 3).reshape(12, 128, 1536))
    g["woutT"] = _ktile(np.ascontiguousarray(np.asarray(inp["L1_attn_W_out"], np.float32).T))
    cw = np.asarray(inp["L1_attn_conv_w"], np.float32)[:, 0, :]
    for k in range(3):
        g[f"convw{k}"] = np.ascontiguousarray(cw[:, k].reshape(12, 128).T)
    g["convb"] = np.ascontiguousarray(np.asarray(inp["L1_attn_conv_b"], np.float32).reshape(12, 128).T)
    g["dtb"] = np.ascontiguousarray(np.asarray(inp["L1_attn_dt_b"], np.float32).reshape(12, 128).T)
    g["A_"] = np.ascontiguousarray(np.asarray(inp["L1_attn_A"], np.float32).reshape(12, 128).T)
    g["Dp"] = np.ascontiguousarray(np.asarray(inp["L1_attn_Dp"], np.float32).reshape(12, 128).T)
    w1c = np.concatenate([np.asarray(inp["L1_ffn_w1"], np.float32),
                          np.asarray(inp["L1_ffn_sw1"], np.float32)], 0)
    w3c = np.concatenate([np.asarray(inp["L1_ffn_w3"], np.float32),
                          np.asarray(inp["L1_ffn_sw3"], np.float32)], 0)
    w2c = np.concatenate([np.asarray(inp["L1_ffn_w2"], np.float32),
                          np.asarray(inp["L1_ffn_sw2"], np.float32)], 0)

    def _sl16(w):
        kt = _ktile(np.ascontiguousarray(w.T))
        return np.ascontiguousarray(
            kt.reshape(128, 8, 16, 128).transpose(2, 0, 1, 3).reshape(16, 128, 1024))
    g["w1_sl"] = np.ascontiguousarray(np.stack([_sl16(w1c[e]) for e in range(NEXP)]))
    g["w3_sl"] = np.ascontiguousarray(np.stack([_sl16(w3c[e]) for e in range(NEXP)]))
    g["w2T"] = np.ascontiguousarray(
        np.stack([_ktile(np.ascontiguousarray(w2c[e].T)) for e in range(NEXP)]))
    g["routerT"] = _ktile(np.ascontiguousarray(np.asarray(inp["L1_ffn_router_w"], np.float32).T))
    g["ebias"] = np.asarray(inp["L1_ffn_expert_bias"], np.float32)[None, :].copy()

    g["identf"] = np.eye(128, dtype=np.float32)
    g["identb"] = np.eye(128, dtype=np.float32).astype(BF)
    g["mq_tab"] = np.ascontiguousarray(_rot_table(np.arange(256)).reshape(2, 128, 1024))
    g["mc_tab"] = np.ascontiguousarray(_rot_table(np.arange(64)))

    x0 = np.asarray(inp["L0_attn_x"], dtype=np.float32)
    per = []
    for core in range(8):
        b, h = core // 2, core % 2
        own = x0[b, 2048 * h:2048 * h + 2048]
        oth = x0[b, 2048 * (1 - h):2048 * (1 - h) + 2048]
        d = dict(g)
        d["xseq"] = np.ascontiguousarray(
            np.concatenate([own, oth], 0).reshape(32, 128, 1024))
        m64 = np.zeros((32, 128, 64), np.float32)
        for j in range(32):
            ci = 16 * h + j if j < 16 else 16 * (1 - h) + (j - 16)
            m64[j, :64, 2 * ci] = 1.0
            m64[j, 64:, 2 * ci + 1] = 1.0
        d["mask64"] = m64.astype(BF)
        rows = []
        for j in range(16):
            g_abs = 8 * h + j // 2
            pos = 4 * g_abs + (j % 2) * 128 + np.arange(128)
            rows.append(_rot_table(pos))
        d["mkl_tab"] = np.ascontiguousarray(np.stack(rows))
        am = np.zeros((8, 2, 128, 316), np.float32)
        for gl in range(8):
            g_abs = 8 * h + gl
            am[gl, :, :, :60] = np.where(
                np.arange(60)[None, None, :] < 4 * g_abs, 0.0, NEG)
            for qc in range(2):
                qpos = 128 * qc + np.arange(128)
                am[gl, qc, :, 60:] = np.where(
                    np.arange(256)[None, :] <= qpos[:, None], 0.0, NEG)
        d["mask_att"] = am.astype(BF)
        d["mh1"] = np.full((128, 1), float(h == 1), np.float32)
        d["mh0"] = np.full((128, 1), float(h == 0), np.float32)
        per.append(d)
    return per



class Sem:
    __slots__ = ("h", "count")

    def __init__(self, h):
        self.h = h
        self.count = 0

    def guard(self):
        return (self, self.count)


class _Inc:
    __slots__ = ("item",)

    def __init__(self, item):
        self.item = item

    def inc(self, sem, n=1):
        self.item[3].append((sem, n))
        sem.count += n
        return self


class Rec:
    def __init__(self, prog, eng):
        self._p = prog
        self._e = eng

    def __getattr__(self, op):
        def f(*args, **kw):
            item = [op, args, kw, []]
            self._p[self._e].append(item)
            return _Inc(item)
        return f

    def wait(self, sem, val=None):
        if val is None:
            val = sem.count
        assert val <= sem.count, "wait on future semaphore value"
        if val > 0:
            self._p[self._e].append(["__wait__", (sem, val), {}, []])

    def wg(self, g):
        if g is not None:
            self.wait(g[0], g[1])


def _replay(prog, eng_name, eng_obj):
    for op, args, kw, incs in prog[eng_name]:
        if op == "__wait__":
            eng_obj.wait_ge(args[0].h, args[1])
            continue
        inst = getattr(eng_obj, op)(*args, **kw)
        for sem, n in incs:
            inst.then_inc(sem.h, n)



class MK:
    def __init__(self, nc, stages):
        self.nc = nc
        self.stages = stages
        self.prog = {e: [] for e in ENGS}
        self.S = Rec(self.prog, "sync")
        self.G = Rec(self.prog, "gpsimd")
        self.V = Rec(self.prog, "vector")
        self.C = Rec(self.prog, "scalar")
        self.T = Rec(self.prog, "tensor")
        self.recs = {"sync": self.S, "gpsimd": self.G, "vector": self.V,
                     "scalar": self.C, "tensor": self.T}
        self.flex_off = 163840
        self.bank_free = [None] * 8
        self.buf = {}

    def at(self, name, shape, dtype, off):
        off = off + SBUF_BASE
        end = off + int(np.prod(shape[1:])) * mybir.dt.size(dtype)
        assert end <= SBUF_LIMIT, f"{name}: sbuf overflow {end}"
        return self.nc.alloc_sbuf_tensor_at(name, list(shape), dtype, offset=off)

    def flex(self, name, shape, dtype):
        off = (self.flex_off + 31) // 32 * 32
        t = self.at(name, shape, dtype, off)
        self.flex_off = off + int(np.prod(shape[1:])) * mybir.dt.size(dtype)
        return t

    FLEX_LIMIT = SBUF_LIMIT - SBUF_BASE

    def barrier(self):
        for e in ENGS:
            self.recs[e].drain().inc(self.bsem[e], 1)
        tgt = {e: self.bsem[e].count for e in ENGS}
        for e in ENGS:
            for o in ENGS:
                if o != e:
                    self.recs[e].wait(self.bsem[o], tgt[o])
        self.bank_free = [None] * 8

    def ln_stats(self, src_ap, sq_dump, b=0):
        V, C, st = self.V, self.C, self.stat
        sV, sC = self.sems["sV"], self.sems["sC"]
        V.reduce_sum(st[:, b:b + 1], src_ap, axis=AX.X)
        V.drain()
        V.tensor_scalar(st[:, b + 8:b + 9], st[:, b:b + 1], 1.0 / 1024, None,
                        ALU.mult).inc(sV, 1)
        C.activation(sq_dump, src_ap, AF.Square, accum_out=st[:, b + 1:b + 2]).inc(sC, 1)
        V.wait(sC)
        V.drain()
        V.tensor_scalar(st[:, b + 2:b + 3], st[:, b + 1:b + 2], 1.0 / 1024, None, ALU.mult)
        V.drain()
        V.tensor_tensor(st[:, b + 3:b + 4], st[:, b + 8:b + 9], st[:, b + 8:b + 9], ALU.mult)
        V.drain()
        V.tensor_tensor(st[:, b + 2:b + 3], st[:, b + 2:b + 3], st[:, b + 3:b + 4],
                        ALU.subtract).inc(sV, 1)
        C.wait(sV)
        C.activation(st[:, b + 4:b + 5], st[:, b + 2:b + 3], AF.Ln,
                     bias=self.epsc[:, 0:1])
        C.drain()
        C.activation(st[:, b + 6:b + 7], st[:, b + 4:b + 5], AF.Exp,
                     scale=-0.5).inc(sC, 1)
        V.wait(sC)
        V.scalar_tensor_tensor(st[:, b + 5:b + 6], st[:, b + 8:b + 9], -1.0,
                               st[:, b + 6:b + 7], ALU.mult, ALU.mult).inc(sV, 1)
        return sV.guard()

    def ln_norm(self, src_ap, out_ap, g_stats, scale=None, bias=None, b=0):
        C, st = self.C, self.stat
        sC = self.sems["sC"]
        C.wg(g_stats)
        C.activation(out_ap, src_ap, AF.Identity,
                     bias=st[:, b + 5:b + 6] if bias is None else bias,
                     scale=st[:, b + 6:b + 7] if scale is None else scale).inc(sC, 1)
        return sC.guard()

    def tset(self, src, dst_fn, n, g_src, g_dst_free=None, src_col0=0):
        T, V = self.T, self.V
        sT, sV = self.sems["sT"], self.sems["sV"]
        PB, identb = self.PB, self.identb
        for i0 in range(0, n, 8):
            cnt = min(8, n - i0)
            T.wg(g_src)
            T.wg(getattr(self, "g_idb", None))
            T.wg(self.bank_free[7])
            hnd = None
            for i in range(cnt):
                hnd = T.transpose(PB[:, 128 * i:128 * (i + 1)],
                                  src[:, src_col0 + 128 * (i0 + i):src_col0 + 128 * (i0 + i + 1)],
                                  identb[:, :])
            hnd.inc(sT, 1)
            V.wait(sT)
            V.wg(g_dst_free)
            h2 = None
            for i in range(cnt):
                h2 = V.tensor_copy(dst_fn(i0 + i), PB[:, 128 * i:128 * (i + 1)])
            h2.inc(sV, 1)
            self.bank_free[7] = sV.guard()
        return sV.guard()


def build_program(stages=4):
    nc = bass.Bass("TRN2", target_bir_lowering=False)
    mk = MK(nc, stages)

    def din(name, shape, dt=F32):
        return nc.dram_tensor(name, list(shape), dt, kind="ExternalInput").ap()

    xseq = din("xseq", (32, 128, 1024))
    wqT_d = din("wqT", (128, 8 * 1024))
    wkvT_d = din("wkvT", (128, 8 * 2048))
    bq_d = din("bq", (1, 1024))
    bkv_d = din("bkv", (1, 2048))
    wprojT_d = din("wprojT", (128, 8 * 1024))
    bproj_d = din("bproj", (1, 1024))
    wassignT_d = din("wassignT", (128, 8 * 64))
    wgF_d = din("wgF", (128, 8 * 4096))
    wuF_d = din("wuF", (128, 8 * 4096))
    wdF_d = din("wdF", (128, 32 * 1024))
    winT_sl_d = din("winT_sl", (24, 128, 1024))
    dtwT_sl_d = din("dtwT_sl", (12, 128, 1536))
    woutT_d = din("woutT", (128, 12 * 1024))
    convw_d = [din(f"convw{k}", (128, 12)) for k in range(3)]
    convb_d = din("convb", (128, 12))
    dtb_d = din("dtb", (128, 12))
    A_d = din("A_", (128, 12))
    Dp_d = din("Dp", (128, 12))
    w1_sl_d = din("w1_sl", (NEXP, 16, 128, 1024))
    w3_sl_d = din("w3_sl", (NEXP, 16, 128, 1024))
    w2T_d = din("w2T", (NEXP, 128, 16 * 1024))
    routerT_d = din("routerT", (128, 8 * 8))
    ebias_d = din("ebias", (1, 8))
    identf_d = din("identf", (128, 128))
    identb_d = din("identb", (128, 128), BF16)
    mq_d = din("mq_tab", (2, 128, 1024), BF16)
    mc_d = din("mc_tab", (64, 1024), BF16)
    mask64_d = din("mask64", (32, 128, 64), BF16)
    mkl_d = din("mkl_tab", (16, 128, 1024), BF16)
    matt_d = din("mask_att", (8, 2, 128, 316), BF16)
    mh1_d = din("mh1", (128, 1))
    mh0_d = din("mh0", (128, 1))
    out_d = nc.dram_tensor("out", [2048, 1024], F32, kind="ExternalOutput").ap()

    wgq_s = nc.dram_tensor("wgq_s", [128, 8 * 4096], BF16).ap()
    win_bf = nc.dram_tensor("win_bf", [24, 128, 1024], BF16).ap()
    dtw_bf = nc.dram_tensor("dtw_bf", [12, 128, 1536], BF16).ap()
    wuq_s = nc.dram_tensor("wuq_s", [128, 8 * 4096], BF16).ap()
    wdq_s = nc.dram_tensor("wdq_s", [128, 32 * 1024], BF16).ap()
    ckp_in = nc.dram_tensor("ckp_in", [64, 2052], F32)
    ckp_out = nc.dram_tensor("ckp_out", [64, 2052], F32)
    edge_in = nc.dram_tensor("edge_in", [24, 128], F32)
    edge_out = nc.dram_tensor("edge_out", [48, 128], F32)
    car_in = nc.dram_tensor("car_in", [12, 128], F32)
    car_out = nc.dram_tensor("car_out", [24, 128], F32)

    from contextlib import ExitStack
    ctx = ExitStack()
    P = [ctx.enter_context(nc.psum_tensor(f"p{i}", [128, 512], F32)) for i in range(7)]
    PB = ctx.enter_context(nc.psum_tensor("pb", [128, 1024], BF16))
    sems = {}
    for sname in ("sS", "sG", "sT", "sV", "sC", "sCC",
                  "bS", "bG", "bT", "bV", "bC"):
        sems[sname] = Sem(ctx.enter_context(nc.semaphore(sname)))
    _pool = [Sem(ctx.enter_context(nc.semaphore(f"dq{i}"))) for i in range(34)]
    _pi = [0]

    def dsem():
        _pi[0] += 1
        return _pool[_pi[0] - 1]
    mk.dsem = dsem
    mk.sems = sems
    sS, sG, sT, sV, sC, sCC = (sems[k] for k in ("sS", "sG", "sT", "sV", "sC", "sCC"))
    mk.bsem = {"sync": sems["bS"], "gpsimd": sems["bG"], "tensor": sems["bT"],
               "vector": sems["bV"], "scalar": sems["bC"]}
    mk.PB = PB
    S, G, V, C, T = mk.S, mk.G, mk.V, mk.C, mk.T
    bank_free = mk.bank_free
    buf = mk.buf

    R = mk.at("R", [128, 16384], F32, 0)
    AR = 65536
    identf = mk.flex("identf", [128, 128], F32)
    identb = mk.flex("identb", [128, 128], BF16)
    mk.identb = identb
    onesb = mk.flex("onesb", [128, 128], BF16)
    stat = mk.flex("stat", [128, 64], F32)
    mk.stat = stat
    mu_t = mk.flex("mu", [128, 16], F32)
    rs_t = mk.flex("rstd", [128, 16], F32)
    mh1 = mk.flex("mh1", [128, 1], F32)
    mh0 = mk.flex("mh0", [128, 1], F32)
    epsc = mk.flex("epsc", [128, 1], F32)
    onec = mk.flex("onec", [128, 1], F32)
    mk.epsc, mk.onec = epsc, onec
    FLEX0 = mk.flex_off
    mk.FLEX0 = FLEX0

    sCst = mk.dsem()
    sR = mk.dsem()
    S.dma_start(out=identf[:, :], in_=identf_d).inc(sCst, 16)
    S.dma_start(out=mh1[:, :], in_=mh1_d).inc(sCst, 16)
    S.dma_start(out=mh0[:, :], in_=mh0_d).inc(sCst, 16)
    sIdb = mk.dsem()
    G.dma_start(out=identb[:, :], in_=identb_d).inc(sIdb, 16)
    g_idb = sIdb.guard()
    mk.g_idb = g_idb
    V.memset(onesb[:, :], 1.0)
    V.memset(epsc[:, :], 1e-5)
    V.memset(onec[:, :], 1.0).inc(sV, 1)
    g_ones = sV.guard()
    S.dma_start(out=R[:, :].rearrange("p (a d) -> p a d", a=16),
                in_=xseq[0:16].transpose([1, 0, 2])).inc(sR, 16)
    g_R = sR.guard()
    g_small = sCst.guard()
    mk.g_idf = g_small

    KRT = mk.at("KRT", [128, 16384], BF16, AR)
    Vn = mk.at("Vn", [128, 16384], BF16, AR + 32768)
    WKV = mk.at("WKV", [128, 16384], BF16, AR + 65536)
    WQ = mk.at("WQ", [128, 8192], BF16, AR + 65536)
    WPJ = mk.at("WPJ", [128, 8192], BF16, AR + 65536 + 16384)

    mk.flex_off = FLEX0
    xn_b = [mk.flex(f"xn{i}", [128, 1024], BF16) for i in range(2)]
    xnt_b = [mk.flex(f"xnt{i}", [128, 1024], BF16) for i in range(2)]
    qbuf = mk.flex("qbuf", [128, 1024], BF16)
    sqd = mk.flex("sqd", [128, 1024], BF16)
    ckcv = mk.flex("ckcv", [64, 2048], BF16)
    kcrT = mk.flex("kcrT", [128, 512], BF16)
    mc_b = mk.flex("mcb", [64, 1024], BF16)
    bias_t = mk.flex("biast", [1, 2048], BF16)
    bkv_b = bias_t
    PH0 = mk.flex_off
    ckp = mk.flex("ckp", [64, 2052], F32)
    kvev = mk.flex("kvev", [128, 2064], BF16)
    mkl_b = [mk.flex(f"mkl{i}", [128, 1024], BF16) for i in range(2)]
    m64_b = [mk.flex(f"m64{i}", [128, 64], BF16) for i in range(2)]
    ew64 = mk.flex("ew64", [128, 64], BF16)
    was_b = mk.flex("wasb", [128, 512], BF16)
    asg_f = mk.flex("asgf", [128, 64], F32)
    assert mk.flex_off + SBUF_BASE <= SBUF_LIMIT, mk.flex_off
    mk.flex_off = PH0
    mq_b = mk.flex("mqb", [128, 2048], BF16)
    matt_b = [mk.flex("matt0", [128, 632], BF16)]
    qrt = [mk.flex(f"qrt{i}", [128, 2048], BF16) for i in range(2)]
    yt = mk.flex("yt", [128, 2048], BF16)
    ep_f = [mk.flex(f"epf{i}", [128, 316], F32) for i in range(2)]
    ep_b = [mk.flex(f"epb{i}", [128, 316], BF16) for i in range(3)]
    pt_b = [mk.flex(f"ptb{i}", [128, 384], BF16) for i in range(2)]
    assert mk.flex_off + SBUF_BASE <= SBUF_LIMIT, mk.flex_off
    bq_b = bias_t[0:1, 0:1024]
    bpj_b = bias_t[0:1, 1024:2048]

    if stages >= 1:
        sWL0 = mk.dsem()
        G.dma_start(out=WKV[:, :], in_=wkvT_d).inc(sWL0, 16)
        G.dma_start(out=bkv_b[:, :], in_=bkv_d).inc(sWL0, 16)
        G.dma_start(out=was_b[:, :], in_=wassignT_d).inc(sWL0, 16)
        G.dma_start(out=mc_b[:, :], in_=mc_d).inc(sWL0, 16)
        g_wkv = sWL0.guard()
        sPC = mk.dsem()
        G.dma_start(out=win_bf, in_=winT_sl_d).inc(sPC, 16)
        G.dma_start(out=dtw_bf, in_=dtwT_sl_d).inc(sPC, 16)
        mk.g_precast = sPC.guard()
        mk.win_bf, mk.dtw_bf = win_bf, dtw_bf
        sXC = [mk.dsem(), mk.dsem()]
        sMKL = [mk.dsem(), mk.dsem()]
        sM64 = [mk.dsem(), mk.dsem()]
        sMT = [mk.dsem(), mk.dsem()]
        sWQg = mk.dsem()

        def pre_chunk(j):
            bi = j % 2
            G.wg(buf.get(("m64", bi)))
            G.dma_start(out=m64_b[bi][:, :], in_=mask64_d[j]).inc(sM64[bi], 16)
            buf[("g_m64", bi)] = sM64[bi].guard()
            S.wg(buf.get(("mkl", bi)))
            S.dma_start(out=mkl_b[bi][:, :], in_=mkl_d[j]).inc(sMKL[bi], 16)
            buf[("g_mkl", bi)] = sMKL[bi].guard()
            src = R[:, 1024 * j:1024 * (j + 1)]
            lb = 16 + 10 * bi
            V.wg(g_R)
            C.wg(g_R)
            g_st = mk.ln_stats(src, sqd[:, :], b=lb)
            V.tensor_copy(mu_t[:, j:j + 1], stat[:, lb + 8:lb + 9])
            V.tensor_copy(rs_t[:, j:j + 1], stat[:, lb + 6:lb + 7]).inc(sV, 1)
            C.wg(buf.get(("xn", bi)))
            g_xn = mk.ln_norm(src, xn_b[bi][:, :], g_st, b=lb)
            buf[("g_xn", bi)] = g_xn

        def main_chunk(j):
            bi = j % 2
            g_xn = buf[("g_xn", bi)]
            g_xnt = mk.tset(xn_b[bi], lambda i, b_=bi: xnt_b[b_][:, 128 * i:128 * (i + 1)],
                            8, g_xn, g_dst_free=buf.get(("xnt", bi)))
            buf[("xn", bi)] = sT.guard()
            for n in range(4):
                pk = 5 + (n % 2)
                T.wg(g_xnt)
                T.wg(g_wkv)
                T.wg(bank_free[pk])
                if n == 0:
                    T.wg(buf.get(("kvev",)))
                for k in range(8):
                    T.matmul(P[pk][:, :], xnt_b[bi][:, 128 * k:128 * (k + 1)],
                             WKV[:, 2048 * k + 512 * n:2048 * k + 512 * (n + 1)],
                             start=(k == 0), stop=False)
                T.matmul(P[pk][:, :], onesb[0:1, :], bkv_b[0:1, 512 * n:512 * (n + 1)],
                         start=False, stop=True).inc(sT, 1)
                V.wait(sT)
                V.tensor_copy(kvev[:, 512 * n:512 * (n + 1)], P[pk][:, :]).inc(sV, 1)
                bank_free[pk] = sV.guard()
            if j == 0:
                V.memset(kvev[:, 2048:2064], 0.0)
                V.drain()
                V.memset(kvev[:, 2048:2049], 1.0).inc(sV, 1)
            g_kvev = sV.guard()
            T.wg(bank_free[6])
            for k in range(8):
                hnd = T.matmul(P[6][:, 0:64], xnt_b[bi][:, 128 * k:128 * (k + 1)],
                               was_b[:, 64 * k:64 * (k + 1)],
                               start=(k == 0), stop=(k == 7))
            hnd.inc(sT, 1)
            buf[("xnt", bi)] = sT.guard()
            V.wait(sT)
            V.tensor_copy(asg_f[:, :], P[6][:, 0:64]).inc(sV, 1)
            bank_free[6] = sV.guard()
            V.wg(buf.get(("g_m64", bi)))
            V.drain()
            V.tensor_tensor(asg_f[:, :], asg_f[:, :], m64_b[bi][:, :], ALU.mult)
            V.drain()
            V.reduce_sum(stat[:, 36:37], asg_f[:, :], axis=AX.X).inc(sV, 1)
            C.wait(sV)
            C.activation(stat[:, 37:38], stat[:, 36:37], AF.Exp).inc(sC, 1)
            V.wait(sC)
            V.tensor_scalar(ew64[:, :], m64_b[bi][:, :], stat[:, 37:38], None,
                            ALU.mult).inc(sV, 1)
            buf[("m64", bi)] = sV.guard()
            g_ew = sV.guard()
            T.wg(g_ew)
            T.wg(g_kvev)
            if j == 0:
                for pk in range(5):
                    T.wg(bank_free[pk])
            for n in range(4):
                T.matmul(P[n][0:64, :], ew64[:, :], kvev[:, 512 * n:512 * (n + 1)],
                         start=(j == 0), stop=(j == 15))
            hnd = T.matmul(P[4][0:64, 0:1], ew64[:, :], kvev[:, 2048:2049],
                           start=(j == 0), stop=(j == 15))
            hnd.inc(sT, 1)
            buf[("kvev",)] = sT.guard()
            V.tensor_copy(Vn[:, 1024 * j:1024 * (j + 1)], kvev[:, 1024:2048]).inc(sV, 1)
            V.wg(buf.get(("g_mkl", bi)))
            V.wg(buf.get(("qbuf",)))
            V.tensor_tensor(qbuf[:, 0:1024:2], kvev[:, 1:1024:2],
                            mkl_b[bi][:, 0:1024:2], ALU.mult)
            V.tensor_tensor(qbuf[:, 1:1024:2], kvev[:, 0:1024:2],
                            mkl_b[bi][:, 1:1024:2], ALU.mult).inc(sV, 1)
            buf[("mkl", bi)] = sV.guard()
            g_kr = sV.guard()
            mk.tset(qbuf, lambda i, j_=j: KRT[:, 2048 * i + 128 * j_:2048 * i + 128 * (j_ + 1)],
                    8, g_kr)
            buf[("qbuf",)] = sT.guard()

        for j in range(17):
            if j < 16:
                pre_chunk(j)
            if j >= 1:
                main_chunk(j - 1)

        g_sum = sT.guard()
        sCK = mk.dsem()
        V.wg(g_sum)
        for n in range(4):
            V.tensor_copy(ckp[0:64, 512 * n:512 * (n + 1)], P[n][0:64, :])
        V.tensor_copy(ckp[0:64, 2048:2049], P[4][0:64, 0:1])
        hnd = V.memset(ckp[0:64, 2049:2052], 0.0)
        hnd.inc(sV, 1)
        for pk in range(5):
            bank_free[pk] = sV.guard()
        G.wait(sV)
        G.dma_start(out=ckp_in.ap(), in_=ckp[0:64, :]).inc(sCK, 16)
        G.wg(sCK.guard())
        G.collective_compute("AllReduce", ALU.add,
                             replica_groups=[[0, 1], [2, 3], [4, 5], [6, 7]],
                             ins=[ckp_in.ap()], outs=[ckp_out.ap()]).inc(sCC, 1)
        G.wait(sCC)
        G.dma_start(out=ckp[0:64, :], in_=ckp_out.ap()).inc(sCK, 16)
        g_ckb = sCK.guard()
        V.wg(g_ckb)
        V.reciprocal(stat[0:64, 38:39], ckp[0:64, 2048:2049])
        V.drain()
        for n in range(4):
            hnd = V.tensor_scalar(ckcv[0:64, 512 * n:512 * (n + 1)], ckp[0:64, 512 * n:512 * (n + 1)],
                                  stat[0:64, 38:39], None, ALU.mult)
        hnd.inc(sV, 1)
        V.drain()
        V.wg(buf.get(("qbuf",)))
        V.tensor_tensor(qbuf[0:64, 0:1024:2], ckcv[0:64, 1:1024:2],
                        mc_b[0:64, 0:1024:2], ALU.mult)
        V.tensor_tensor(qbuf[0:64, 1:1024:2], ckcv[0:64, 0:1024:2],
                        mc_b[0:64, 1:1024:2], ALU.mult).inc(sV, 1)
        g_kcr = sV.guard()
        T.wg(g_kcr)
        T.wg(bank_free[7])
        for i in range(8):
            hnd = T.transpose(PB[:, 64 * i:64 * (i + 1)],
                              qbuf[0:64, 128 * i:128 * (i + 1)], identb[0:64, 0:64])
        hnd.inc(sT, 1)
        V.wait(sT)
        V.tensor_copy(kcrT[:, :], PB[:, 0:512]).inc(sV, 1)
        bank_free[7] = sV.guard()
        g_kcrT = sV.guard()

        g_ph0v = sV.guard()
        g_ph0c = sC.guard()
        G.wg(g_sum)
        G.wg(g_ph0v)
        G.wg(g_ph0c)
        G.dma_start(out=WQ[:, :], in_=wqT_d).inc(sWQg, 16)
        G.dma_start(out=WPJ[:, :], in_=wprojT_d).inc(sWQg, 16)
        G.dma_start(out=bias_t[0:1, 0:1024], in_=bq_d).inc(sWQg, 16)
        G.dma_start(out=bias_t[0:1, 1024:2048], in_=bproj_d).inc(sWQg, 16)
        G.dma_start(out=mq_b[:, :].rearrange("p (a d) -> p a d", a=2),
                    in_=mq_d.transpose([1, 0, 2])).inc(sWQg, 16)
        g_wq = sWQg.guard()

        for gl in range(8):
            mbi = 0
            if gl < 2:
                S.wg(g_ph0v)
                S.wg(g_ph0c)
                S.wg(g_sum)
            S.wg(buf.get(("matt", mbi)))
            S.dma_start(out=matt_b[mbi][:, :].rearrange("p (a d) -> p a d", a=2),
                        in_=matt_d[gl].transpose([1, 0, 2])).inc(sMT[mbi], 16)
            g_mask = sMT[mbi].guard()
            for qc in range(2):
                j = 2 * gl + qc
                src = R[:, 1024 * j:1024 * (j + 1)]
                V.wg(buf.get(("lnb",)))
                V.scalar_tensor_tensor(stat[:, 5:6], mu_t[:, j:j + 1], -1.0,
                                       rs_t[:, j:j + 1], ALU.mult, ALU.mult).inc(sV, 1)
                C.wait(sV)
                C.wg(buf.get(("xn", 0)))
                C.activation(xn_b[0][:, :], src, AF.Identity,
                             bias=stat[:, 5:6], scale=rs_t[:, j:j + 1]).inc(sC, 1)
                buf[("lnb",)] = sC.guard()
                g_xn = sC.guard()
                g_xnt = mk.tset(xn_b[0], lambda i: xnt_b[0][:, 128 * i:128 * (i + 1)],
                                8, g_xn, g_dst_free=buf.get(("xnt", 0)))
                buf[("xn", 0)] = sT.guard()
                for n in range(2):
                    T.wg(g_xnt)
                    T.wg(g_wq)
                    T.wg(bank_free[n])
                    for k in range(8):
                        T.matmul(P[n][:, :], xnt_b[0][:, 128 * k:128 * (k + 1)],
                                 WQ[:, 1024 * k + 512 * n:1024 * k + 512 * (n + 1)],
                                 start=(k == 0), stop=False)
                    hnd = T.matmul(P[n][:, :], onesb[0:1, :], bq_b[:, 512 * n:512 * (n + 1)],
                                   start=False, stop=True)
                hnd.inc(sT, 1)
                buf[("xnt", 0)] = sT.guard()
                V.wait(sT)
                V.wg(buf.get(("qbuf",)))
                V.tensor_copy(qbuf[:, 0:512], P[0][:, :])
                V.tensor_copy(qbuf[:, 512:1024], P[1][:, :]).inc(sV, 1)
                bank_free[0] = sV.guard()
                bank_free[1] = sV.guard()
                V.drain()
                V.wg(g_wq)
                V.wg(buf.get(("xn1",)))
                V.scalar_tensor_tensor(xn_b[1][:, 0:1024:2], qbuf[:, 1:1024:2], 0.125,
                                       mq_b[:, 1024 * qc:1024 * (qc + 1)][:, 0:1024:2],
                                       ALU.mult, ALU.mult)
                V.scalar_tensor_tensor(xn_b[1][:, 1:1024:2], qbuf[:, 0:1024:2], 0.125,
                                       mq_b[:, 1024 * qc:1024 * (qc + 1)][:, 1:1024:2],
                                       ALU.mult, ALU.mult).inc(sV, 1)
                buf[("qbuf",)] = sV.guard()
                g_qr = sV.guard()
                mk.tset(xn_b[1], lambda i, q_=qc: qrt[q_][:, 128 * i:128 * (i + 1)],
                        8, g_qr, g_dst_free=buf.get(("qrt", qc)))
                buf[("xn1",)] = sT.guard()
            g_qrt = sV.guard()

            NIT = 32
            g_sc = [None] * NIT
            g_m = [None] * NIT
            g_e = [None] * NIT
            g_n = [None] * NIT
            g_tr = [None] * NIT
            g_pc = [None] * NIT
            g_pv = [None] * NIT

            def it(i):
                qc, hh = i // 16, i % 16
                return qc, hh, hh // 2, 64 * (hh % 2), i % 6, i % 2, i % 3

            for tick in range(NIT + 6):
                i = tick - 6
                if 0 <= i < NIT:
                    qc, hh, dt_, po, pk6, b2, b3 = it(i)
                    C.wait(sT, g_pv[i][1])
                    C.copy(yt[po:po + 64, 256 * dt_ + 128 * qc:256 * dt_ + 128 * (qc + 1)],
                           P[pk6][po:po + 64, 384:512]).inc(sC, 1)
                    bank_free[pk6] = sC.guard()
                i = tick - 5
                if 0 <= i < NIT:
                    qc, hh, dt_, po, pk6, b2, b3 = it(i)
                    pbo = 384 * b2
                    C.wait(sT, g_tr[i][1])
                    C.wg(g_pv[i - 2] if i >= 2 else None)
                    C.copy(pt_b[b2][:, :], PB[:, pbo:pbo + 384]).inc(sC, 1)
                    g_pc[i] = sC.guard()
                    T.wg(g_pc[i])
                    T.matmul(P[pk6][po:po + 64, 384:512],
                             ckcv[0:60, 1024 + 64 * hh:1024 + 64 * (hh + 1)],
                             pt_b[b2][0:60, 0:128], start=True, stop=False)
                    T.matmul(P[pk6][po:po + 64, 384:512],
                             Vn[:, 2048 * gl + 64 * hh:2048 * gl + 64 * (hh + 1)],
                             pt_b[b2][:, 128:256], start=False, stop=False)
                    T.matmul(P[pk6][po:po + 64, 384:512],
                             Vn[:, 2048 * gl + 1024 + 64 * hh:2048 * gl + 1024 + 64 * (hh + 1)],
                             pt_b[b2][:, 256:384], start=False, stop=True).inc(sT, 1)
                    g_pv[i] = sT.guard()
                i = tick - 4
                if 0 <= i < NIT:
                    qc, hh, dt_, po, pk6, b2, b3 = it(i)
                    pbo = 384 * b2
                    T.wg(g_n[i])
                    T.wg(g_pc[i - 2] if i >= 2 else None)
                    T.transpose(PB[0:60, pbo:pbo + 128], ep_b[b3][0:128, 0:60], identb[:, :])
                    T.transpose(PB[0:128, pbo + 128:pbo + 256], ep_b[b3][:, 60:188],
                                identb[:, :])
                    T.transpose(PB[0:128, pbo + 256:pbo + 384], ep_b[b3][:, 188:316],
                                identb[:, :]).inc(sT, 1)
                    g_tr[i] = sT.guard()
                i = tick - 3
                if 0 <= i < NIT:
                    qc, hh, dt_, po, pk6, b2, b3 = it(i)
                    V.wg(g_e[i])
                    V.reciprocal(stat[:, 44 + b3:45 + b3], stat[:, 40 + b3:41 + b3])
                    V.drain()
                    V.tensor_scalar(ep_b[b3][:, :], ep_b[b3][:, :],
                                    stat[:, 44 + b3:45 + b3], None, ALU.mult).inc(sV, 1)
                    g_n[i] = sV.guard()
                i = tick - 2
                if 0 <= i < NIT:
                    qc, hh, dt_, po, pk6, b2, b3 = it(i)
                    C.wg(g_m[i])
                    C.wg(g_tr[i - 3] if i >= 3 else None)
                    C.activation(ep_b[b3][:, :], ep_f[b2][:, :], AF.Exp,
                                 accum_out=stat[:, 40 + b3:41 + b3]).inc(sC, 1)
                    g_e[i] = sC.guard()
                i = tick - 1
                if 0 <= i < NIT:
                    qc, hh, dt_, po, pk6, b2, b3 = it(i)
                    V.wg(g_sc[i])
                    V.wg(g_mask)
                    V.wg(g_e[i - 2] if i >= 2 else None)
                    V.scalar_tensor_tensor(ep_f[b2][:, :], P[pk6][:, 0:316], 80.0,
                                           matt_b[mbi][:, 316 * qc:316 * (qc + 1)],
                                           ALU.min, ALU.add).inc(sV, 1)
                    g_m[i] = sV.guard()
                if tick < NIT:
                    qc, hh, dt_, po, pk6, b2, b3 = it(tick)
                    T.wg(g_qrt)
                    if tick == 0:
                        T.wg(g_kcrT)
                        T.wg(buf.get(("yt",)))
                    T.wg(bank_free[pk6])
                    T.matmul(P[pk6][0:128, 0:60],
                             qrt[qc][po:po + 64, 128 * dt_:128 * (dt_ + 1)],
                             kcrT[po:po + 64, 64 * dt_:64 * dt_ + 60],
                             start=True, stop=True)
                    T.matmul(P[pk6][0:128, 60:316],
                             qrt[qc][po:po + 64, 128 * dt_:128 * (dt_ + 1)],
                             KRT[po:po + 64, 2048 * dt_ + 256 * gl:2048 * dt_ + 256 * (gl + 1)],
                             start=True, stop=True).inc(sT, 1)
                    g_sc[tick] = sT.guard()
            buf[("matt", mbi)] = sV.guard()
            buf[("qrt", 0)] = sT.guard()
            buf[("qrt", 1)] = sT.guard()
            g_yt = sC.guard()
            for qc in range(2):
                j = 2 * gl + qc
                for n in range(2):
                    T.wg(g_yt)
                    T.wg(g_wq)
                    T.wg(bank_free[n])
                    for k in range(8):
                        T.matmul(P[n][:, :], yt[:, 256 * k + 128 * qc:256 * k + 128 * (qc + 1)],
                                 WPJ[:, 1024 * k + 512 * n:1024 * k + 512 * (n + 1)],
                                 start=(k == 0), stop=False)
                    hnd = T.matmul(P[n][:, :], onesb[0:1, :], bpj_b[:, 512 * n:512 * (n + 1)],
                                   start=False, stop=True)
                hnd.inc(sT, 1)
                if qc == 1:
                    buf[("yt",)] = sT.guard()
                V.wait(sT)
                V.tensor_tensor(R[:, 1024 * j:1024 * j + 512], P[0][:, :],
                                R[:, 1024 * j:1024 * j + 512], ALU.add)
                V.tensor_tensor(R[:, 1024 * j + 512:1024 * (j + 1)], P[1][:, :],
                                R[:, 1024 * j + 512:1024 * (j + 1)], ALU.add).inc(sV, 1)
                bank_free[0] = sV.guard()
                bank_free[1] = sV.guard()
        mk.barrier()

    if stages >= 2:
        emit_l0ffn(mk, R, P, PB, AR, identb, onesb,
                   wgF_d, wuF_d, wdF_d, wgq_s, wuq_s, wdq_s, identf)
        mk.barrier()

    if stages >= 3:
        emit_l1attn(mk, R, P, PB, AR, identb, identf, winT_sl_d, dtwT_sl_d,
                    woutT_d, convw_d, convb_d, dtb_d, A_d, Dp_d,
                    edge_in, edge_out, car_in, car_out, mh1, mh0)
        mk.barrier()

    if stages >= 4:
        emit_l1ffn(mk, R, P, PB, AR, identb, onesb,
                   w1_sl_d, w3_sl_d, w2T_d, routerT_d, ebias_d)
        mk.barrier()

    S.wait(sV, sV.count)
    S.dma_start(out=out_d.rearrange("(a p) d -> a p d", p=128).transpose([1, 0, 2]),
                in_=R[:, :].rearrange("p (a d) -> p a d", a=16)).inc(sS, 16)
    S.wait(sS, sS.count)

    with nc.Block() as block:
        @block.sync
        def _(e):
            _replay(mk.prog, "sync", e)

        @block.gpsimd
        def _(e):
            _replay(mk.prog, "gpsimd", e)

        @block.vector
        def _(e):
            _replay(mk.prog, "vector", e)

        @block.scalar
        def _(e):
            _replay(mk.prog, "scalar", e)

        @block.tensor
        def _(e):
            _replay(mk.prog, "tensor", e)
    ctx.close()
    return nc



def emit_l0ffn(mk, R, P, PB, AR, identb, onesb,
               wgF_d, wuF_d, wdF_d, wgq_s, wuq_s, wdq_s, identf):
    nc = mk.nc
    S, G, V, C, T = mk.S, mk.G, mk.V, mk.C, mk.T
    sems = mk.sems
    sS, sG, sT, sV, sC = (sems[k] for k in ("sS", "sG", "sT", "sV", "sC"))
    stat = mk.stat
    bank_free = mk.bank_free
    buf = mk.buf

    WSF = [mk.at(f"wsf{i}", [128, 4096], F32, AR + 16384 * i) for i in range(2)]
    WQB = [mk.at(f"wqb{i}", [128, 4096], BF16, AR + 32768 + 8192 * i) for i in range(2)]
    XQT = mk.at("XQT", [128, 2048], BF16, AR)
    GS = mk.at("GS", [128, 8192], F32, AR + 4096)
    GQT = mk.at("GQT", [128, 8192], BF16, AR + 4096 + 32768)
    WGB = [mk.at(f"wgb{i}", [128, 4096], BF16, AR + 4096 + 49152 + 8192 * i)
           for i in range(2)]
    WUB = [mk.at(f"wub{i}", [128, 4096], BF16, AR + 4096 + 65536 + 8192 * i)
           for i in range(2)]
    WDB = WGB

    mk.flex_off = mk.FLEX0
    xf = mk.flex("l0f_xf", [128, 1024], F32)
    xq = mk.flex("l0f_xq", [128, 1024], BF16)
    gq = [mk.flex(f"l0f_gq{i}", [128, 4096], BF16) for i in range(2)]
    sil = mk.flex("l0f_sil", [128, 512], F32)
    qst = mk.flex("l0f_qst", [128, 16], F32)
    sqd = mk.flex("l0f_sqd", [128, 1024], BF16)
    onesrow = mk.flex("l0f_ones", [1, 128], F32)
    assert mk.flex_off + SBUF_BASE <= SBUF_LIMIT

    V.memset(onesrow[:, :], 1.0).inc(sV, 1)
    sWSF = [mk.dsem(), mk.dsem()]
    sSCW = [mk.dsem(), mk.dsem()]
    sWGB = [mk.dsem(), mk.dsem()]
    sWUB = [mk.dsem(), mk.dsem()]

    mats = [(wgF_d, wgq_s, 8, 0), (wuF_d, wuq_s, 8, 1), (wdF_d, wdq_s, 8, 2)]
    for (wd, _, nch, mi) in mats:
        V.memset(stat[:, 20 + mi:21 + mi], 0.0).inc(sV, 1)
        for c in range(nch):
            bi = c % 2
            S.wg(buf.get(("wsf", bi)))
            S.dma_start(out=WSF[bi][:, :], in_=wd[:, 4096 * c:4096 * (c + 1)]).inc(sWSF[bi], 16)
            V.wg(sWSF[bi].guard())
            V.reduce_sum(stat[:, 24:25], WSF[bi][:, :], axis=AX.X,
                         apply_absolute_value=True).inc(sV, 1)
            buf[("wsf", bi)] = sV.guard()
            V.drain()
            V.tensor_tensor(stat[:, 20 + mi:21 + mi], stat[:, 20 + mi:21 + mi],
                            stat[:, 24:25], ALU.add)
            V.drain()
        V.drain().inc(sV, 1)
        T.wg(sV.guard())
        T.wg(mk.g_idf)
        T.wg(bank_free[6])
        T.transpose(P[6][0:1, 0:128], stat[:, 20 + mi:21 + mi], identf[:, :]).inc(sT, 1)
        V.wait(sT)
        V.reduce_sum(stat[0:1, 24:25], P[6][0:1, 0:128], axis=AX.X).inc(sV, 1)
        bank_free[6] = sV.guard()
        V.drain()
        V.tensor_scalar(stat[0:1, 24:25], stat[0:1, 24:25], 1.0 / 4194304.0, 1e-5,
                        ALU.mult, ALU.max).inc(sV, 1)
        T.wait(sV)
        T.wg(bank_free[6])
        T.matmul(P[6][:, 0:1], onesrow[0:1, :], stat[0:1, 24:25],
                 start=True, stop=True).inc(sT, 1)
        V.wait(sT)
        V.tensor_copy(qst[:, 2 * mi:2 * mi + 1], P[6][:, 0:1]).inc(sV, 1)
        bank_free[6] = sV.guard()
        V.drain()
        V.reciprocal(qst[:, 2 * mi + 1:2 * mi + 2], qst[:, 2 * mi:2 * mi + 1]).inc(sV, 1)
        V.drain()
    g_tw = sV.guard()

    for (wd, ws, nch, mi) in mats:
        sw = qst[:, 2 * mi + 1:2 * mi + 2]
        for c in range(nch):
            bi = c % 2
            S.wg(buf.get(("wsf", bi)))
            S.dma_start(out=WSF[bi][:, :], in_=wd[:, 4096 * c:4096 * (c + 1)]).inc(sWSF[bi], 16)
            V.wg(sWSF[bi].guard())
            V.wg(buf.get(("wqb", bi)))
            V.tensor_scalar(WSF[bi][:, :], WSF[bi][:, :], sw, MAGIC, ALU.mult, ALU.add)
            V.drain()
            V.tensor_scalar(WSF[bi][:, :], WSF[bi][:, :], MAGIC, -1.0, ALU.subtract, ALU.max)
            V.drain()
            hnd = V.tensor_scalar(WQB[bi][:, :], WSF[bi][:, :], 1.0, None, ALU.min)
            hnd.inc(sV, 1)
            buf[("wsf", bi)] = sV.guard()
            G.wait(sV)
            if mi < 2:
                ws4 = ws.rearrange("p (n k s) -> p n k s", n=8, k=8)
                G.dma_start(out=ws4[:, :, c, :], in_=WQB[bi][:, :]).inc(sSCW[bi], 16)
            else:
                G.dma_start(out=ws[:, 4096 * c:4096 * (c + 1)], in_=WQB[bi][:, :]).inc(sSCW[bi], 16)
            buf[("wqb", bi)] = sSCW[bi].guard()
    g_scr = sSCW[0].guard()
    g_scr2 = sSCW[1].guard()

    sWDB = [mk.dsem(), mk.dsem()]

    for sp in range(8):
        if sp == 0:
            V.wg(g_scr)
            V.wg(g_scr2)
            C.wg(g_scr)
            C.wg(g_scr2)
        for lt2 in range(2):
            lt = 2 * sp + lt2
            src = R[:, 1024 * lt:1024 * (lt + 1)]
            g_st = mk.ln_stats(src, sqd[:, :])
            V.wg(buf.get(("l0f_xf",)))
            g_xf = mk.ln_norm(src, xf[:, :], g_st)
            V.wg(g_xf)
            V.reduce_max(stat[:, 25:26], xf[:, :], axis=AX.X,
                         apply_absolute_value=True).inc(sV, 1)
            V.drain()
            V.tensor_scalar(stat[:, 25:26], stat[:, 25:26], 1e-5, None, ALU.max)
            V.drain()
            V.reciprocal(stat[:, 26:27], stat[:, 25:26])
            V.drain()
            V.tensor_scalar(stat[:, 27:28], stat[:, 26:27], 127.0, None, ALU.mult)
            V.drain()
            V.tensor_scalar(stat[:, 28:29], stat[:, 25:26], 1.0 / 127.0, None, ALU.mult)
            V.drain()
            V.tensor_tensor(qst[:, 8 + lt2:9 + lt2], stat[:, 28:29], qst[:, 0:1], ALU.mult)
            V.tensor_tensor(qst[:, 10 + lt2:11 + lt2], stat[:, 28:29], qst[:, 2:3], ALU.mult)
            V.tensor_copy(qst[:, 14 + lt2:15 + lt2], stat[:, 28:29])
            V.drain()
            V.wg(buf.get(("l0f_xq",)))
            V.tensor_scalar(xf[:, :], xf[:, :], stat[:, 27:28], MAGIC, ALU.mult, ALU.add)
            V.drain()
            V.tensor_scalar(xf[:, :], xf[:, :], MAGIC, -128.0, ALU.subtract, ALU.max)
            V.drain()
            V.tensor_scalar(xq[:, :], xf[:, :], 127.0, None, ALU.min).inc(sV, 1)
            buf[("l0f_xf",)] = sV.guard()
            g_xq = sV.guard()
            mk.tset(xq, lambda i, l_=lt2: XQT[:, 256 * i + 128 * l_:256 * i + 128 * (l_ + 1)],
                    8, g_xq, g_dst_free=buf.get(("l0f_xqt", lt2)))
            buf[("l0f_xq",)] = sT.guard()
        g_xqt = sV.guard()

        def gu_half(lt2):
            for nch in range(8):
                bi = nch % 2
                for (wsrc, wb, tag, sm) in ((wgq_s, WGB, "wgb", sWGB), (wuq_s, WUB, "wub", sWUB)):
                    S.wg(buf.get((tag, bi)))
                    S.wg(g_scr)
                    S.wg(g_scr2)
                    S.dma_start(out=wb[bi][:, :],
                                in_=wsrc[:, 4096 * nch:4096 * (nch + 1)]).inc(sm[bi], 16)
                g_w = (sWGB[bi], sWGB[bi].count)
                g_wu = (sWUB[bi], sWUB[bi].count)
                for (wb, pk0, gw) in ((WGB, 0, g_w), (WUB, 2, g_wu)):
                    pk = pk0 + lt2
                    T.wg(g_xqt)
                    T.wg(gw)
                    T.wg(bank_free[pk])
                    for k in range(8):
                        hnd = T.matmul(P[pk][:, :],
                                       XQT[:, 256 * k + 128 * lt2:256 * k + 128 * (lt2 + 1)],
                                       wb[bi][:, 512 * k:512 * (k + 1)],
                                       start=(k == 0), stop=(k == 7))
                    hnd.inc(sT, 1)
                if nch == 7 and lt2 == 1:
                    for l_ in range(2):
                        buf[("l0f_xqt", l_)] = sT.guard()
                buf[("wgb", bi)] = sT.guard()
                buf[("wub", bi)] = sT.guard()
                C.wait(sT)
                C.wg(buf.get(("l0f_sil",)))
                C.activation(sil[:, :], P[0 + lt2][:, :], AF.Silu,
                             scale=qst[:, 8 + lt2:9 + lt2]).inc(sC, 1)
                V.wait(sC)
                V.scalar_tensor_tensor(GS[:, 4096 * lt2 + 512 * nch:4096 * lt2 + 512 * (nch + 1)],
                                       P[2 + lt2][:, :], qst[:, 10 + lt2:11 + lt2],
                                       sil[:, :], ALU.mult, ALU.mult).inc(sV, 1)
                buf[("l0f_sil",)] = sV.guard()
                bank_free[0 + lt2] = sV.guard()
                bank_free[2 + lt2] = sV.guard()

        def quantv(lt2):
            gsl = GS[:, 4096 * lt2:4096 * (lt2 + 1)]
            V.wg(sV.guard())
            V.reduce_max(stat[:, 25:26], gsl, axis=AX.X, apply_absolute_value=True)
            V.drain()
            V.tensor_scalar(stat[:, 25:26], stat[:, 25:26], 1e-5, None, ALU.max)
            V.drain()
            V.reciprocal(stat[:, 26:27], stat[:, 25:26])
            V.drain()
            V.tensor_scalar(stat[:, 27:28], stat[:, 26:27], 127.0, None, ALU.mult)
            V.drain()
            V.tensor_scalar(stat[:, 28:29], stat[:, 25:26], 1.0 / 127.0, None, ALU.mult)
            V.drain()
            V.tensor_tensor(qst[:, 12 + lt2:13 + lt2], stat[:, 28:29], qst[:, 4:5], ALU.mult)
            V.drain()
            V.tensor_scalar(gsl, gsl, stat[:, 27:28], MAGIC, ALU.mult, ALU.add)
            V.drain()
            V.tensor_scalar(gsl, gsl, MAGIC, -128.0, ALU.subtract, ALU.max)
            V.drain()
            V.wg(buf.get(("l0f_gq", lt2)))
            V.tensor_scalar(gq[lt2][:, :], gsl, 127.0, None, ALU.min).inc(sV, 1)
            return sV.guard()

        def tsetg(lt2, g_gq):
            mk.tset(gq[lt2], lambda i, l_=lt2: GQT[:, 256 * i + 128 * l_:256 * i + 128 * (l_ + 1)],
                    32, g_gq, g_dst_free=buf.get(("l0f_gqt", lt2)))
            buf[("l0f_gq", lt2)] = sT.guard()

        gu_half(0)
        g_gq0 = quantv(0)
        gu_half(1)
        tsetg(0, g_gq0)
        g_gq1 = quantv(1)
        tsetg(1, g_gq1)
        g_gqt = sV.guard()

        for kg in range(8):
            bi = kg % 2
            S.wg(buf.get(("wgb", bi)))
            S.wg(g_scr)
            S.wg(g_scr2)
            S.dma_start(out=WDB[bi][:, :],
                        in_=wdq_s[:, 4096 * kg:4096 * (kg + 1)]).inc(sWDB[bi], 16)
            g_w = sWDB[bi].guard()
            T.wg(g_gqt)
            T.wg(g_w)
            if kg == 0:
                for pk in (0, 1, 4, 5):
                    T.wg(bank_free[pk])
            for k4 in range(4):
                k = 4 * kg + k4
                for lt2 in range(2):
                    for dch in range(2):
                        pk = (4 + dch) if lt2 == 0 else (0 + dch)
                        hnd = T.matmul(P[pk][:, :],
                                       GQT[:, 256 * k + 128 * lt2:256 * k + 128 * (lt2 + 1)],
                                       WDB[bi][:, 1024 * k4 + 512 * dch:1024 * k4 + 512 * (dch + 1)],
                                       start=(k == 0), stop=(k == 31))
            hnd.inc(sT, 1)
            buf[("wgb", bi)] = sT.guard()
        g_dn = sT.guard()
        V.wg(g_dn)
        for lt2 in range(2):
            lt = 2 * sp + lt2
            for dch in range(2):
                cs = 1024 * lt + 512 * dch
                pk = (4 + dch) if lt2 == 0 else (0 + dch)
                hnd = V.scalar_tensor_tensor(R[:, cs:cs + 512], P[pk][:, :],
                                             qst[:, 12 + lt2:13 + lt2], R[:, cs:cs + 512],
                                             ALU.mult, ALU.add)
        hnd.inc(sV, 1)
        for l_ in range(2):
            buf[("l0f_gqt", l_)] = sV.guard()
        for pk in (0, 1, 4, 5):
            bank_free[pk] = sV.guard()





def emit_l1attn(mk, R, P, PB, AR, identb, identf, winT_sl_d, dtwT_sl_d,
                woutT_d, convw_d, convb_d, dtb_d, A_d, Dp_d,
                edge_in, edge_out, car_in, car_out, mh1, mh0):
    nc = mk.nc
    S, G, V, C, T = mk.S, mk.G, mk.V, mk.C, mk.T
    sems = mk.sems
    sS, sG, sT, sV, sC, sCC = (sems[k] for k in ("sS", "sG", "sT", "sV", "sC", "sCC"))
    stat = mk.stat
    bank_free = mk.bank_free
    buf = mk.buf

    XI = mk.at("XI", [128, 24576], BF16, AR)
    SZ = mk.at("SZ", [128, 24576], BF16, AR + 49152)
    WOUT = mk.at("WOUT", [128, 12288], BF16, AR + 49152)

    mk.flex_off = mk.FLEX0
    xn = mk.flex("l1_xn", [128, 1024], BF16)
    xnt = mk.flex("l1_xnt", [128, 4096], BF16)
    wbuf = [mk.flex(f"l1_w{i}", [128, 1536], BF16) for i in range(2)]
    sqd = mk.flex("l1_sqd", [128, 1024], BF16)
    scf0 = mk.flex("l1_s0", [128, 512], F32)
    scf1 = mk.flex("l1_s1", [128, 512], F32)
    scf2 = mk.flex("l1_s2", [128, 512], F32)
    ctmp = mk.flex("l1_ct", [128, 2048], F32)
    p1st = mk.flex("l1_p1", [128, 6144], BF16)
    carry = mk.flex("l1_carry", [128, 12], F32)
    tot = mk.flex("l1_tot", [128, 12], F32)
    haloL = mk.flex("l1_hl", [128, 12], F32)
    haloR = mk.flex("l1_hr", [128, 12], F32)
    ed = mk.flex("l1_ed", [128, 24], F32)
    edT = mk.flex("l1_edT", [24, 128], F32)
    car_sb = mk.flex("l1_carT", [12, 128], F32)
    cw = [mk.flex(f"l1_cw{k}", [128, 12], F32) for k in range(3)]
    cb = mk.flex("l1_cb", [128, 12], F32)
    dtb = mk.flex("l1_dtb", [128, 12], F32)
    Ab = mk.flex("l1_A", [128, 12], F32)
    Dpb = mk.flex("l1_Dp", [128, 12], F32)
    assert mk.flex_off + SBUF_BASE <= SBUF_LIMIT

    sCst1 = mk.dsem()
    sWB1 = [mk.dsem(), mk.dsem()]
    win_bf, dtw_bf = mk.win_bf, mk.dtw_bf
    sGx = mk.dsem()
    sWOg = mk.dsem()
    for k in range(3):
        S.dma_start(out=cw[k][:, :], in_=convw_d[k]).inc(sCst1, 16)
    S.dma_start(out=cb[:, :], in_=convb_d).inc(sCst1, 16)
    S.dma_start(out=dtb[:, :], in_=dtb_d).inc(sCst1, 16)
    S.dma_start(out=Ab[:, :], in_=A_d).inc(sCst1, 16)
    S.dma_start(out=Dpb[:, :], in_=Dp_d).inc(sCst1, 16)
    g_cst = sCst1.guard()

    for lc in range(4):
        for lt2 in range(4):
            lt = 4 * lc + lt2
            src = R[:, 1024 * lt:1024 * (lt + 1)]
            g_st = mk.ln_stats(src, sqd[:, :])
            C.wg(buf.get(("l1_xn",)))
            g_xn = mk.ln_norm(src, xn[:, :], g_st)
            mk.tset(xn, lambda i, l_=lt2: xnt[:, 512 * i + 128 * l_:512 * i + 128 * (l_ + 1)],
                    8, g_xn, g_dst_free=buf.get(("l1_xnt", lt2)))
            buf[("l1_xn",)] = sT.guard()
        g_xnt = sV.guard()
        for et in range(24):
            bi = et % 2
            S.wg(buf.get(("l1_w", bi)))
            if lc == 0 and et < 2:
                S.wg(mk.g_precast)
            S.dma_start(out=wbuf[bi][:, 0:1024], in_=win_bf[et]).inc(sWB1[bi], 16)
            g_w = sWB1[bi].guard()
            pk = et % 6
            T.wg(g_xnt)
            T.wg(g_w)
            T.wg(bank_free[pk])
            for k in range(8):
                hnd = T.matmul(P[pk][:, :], wbuf[bi][:, 128 * k:128 * (k + 1)],
                               xnt[:, 512 * k:512 * (k + 1)],
                               start=(k == 0), stop=(k == 7))
            hnd.inc(sT, 1)
            buf[("l1_w", bi)] = sT.guard()
            if et == 23:
                for l_ in range(4):
                    buf[("l1_xnt", l_)] = sT.guard()
            if et < 12:
                V.wait(sT)
                V.tensor_copy(XI[:, 2048 * et + 512 * lc:2048 * et + 512 * (lc + 1)],
                              P[pk][:, :]).inc(sV, 1)
                bank_free[pk] = sV.guard()
            else:
                C.wait(sT)
                C.activation(SZ[:, 2048 * (et - 12) + 512 * lc:2048 * (et - 12) + 512 * (lc + 1)],
                             P[pk][:, :], AF.Silu).inc(sC, 1)
                bank_free[pk] = sC.guard()
    g_win_v = sV.guard()
    g_win_c = sC.guard()

    V.wg(g_win_v)
    for et in range(12):
        V.tensor_copy(ed[:, et:et + 1], XI[:, 2048 * et:2048 * et + 1])
        V.tensor_copy(ed[:, 12 + et:13 + et], XI[:, 2048 * et + 2047:2048 * et + 2048])
    V.drain().inc(sV, 1)
    g_ed = sV.guard()
    T.wg(g_ed)
    T.wg(bank_free[6])
    T.wg(mk.g_idf)
    T.transpose(P[6][0:24, 0:128], ed[:, :], identf[:, :]).inc(sT, 1)
    V.wait(sT)
    V.tensor_copy(edT[:, :], P[6][0:24, 0:128]).inc(sV, 1)
    bank_free[6] = sV.guard()
    G.wait(sV)
    G.dma_start(out=edge_in[:, :], in_=edT[:, :]).inc(sGx, 16)
    G.wg(sGx.guard())
    G.collective_compute("AllGather", ALU.bypass,
                         replica_groups=[[0, 1], [2, 3], [4, 5], [6, 7]],
                         ins=[edge_in.ap()], outs=[edge_out.ap()]).inc(sCC, 1)
    G.wait(sCC)
    G.dma_start(out=edT[:, :], in_=edge_out[12:36, :]).inc(sGx, 16)
    g_edb = sGx.guard()
    T.wg(g_edb)
    T.wg(bank_free[6])
    T.wg(mk.g_idf)
    T.transpose(P[6][0:128, 0:24], edT[:, :], identf[0:24, 0:24]).inc(sT, 1)
    V.wait(sT)
    V.tensor_scalar(haloL[:, :], P[6][:, 0:12], mh1[:, 0:1], None, ALU.mult)
    V.tensor_scalar(haloR[:, :], P[6][:, 12:24], mh0[:, 0:1], None, ALU.mult).inc(sV, 1)
    bank_free[6] = sV.guard()
    g_halo = sV.guard()

    for et in range(12):
        xi = XI[:, 2048 * et:2048 * (et + 1)]
        V.wg(g_halo)
        V.wg(g_cst)
        V.wg(buf.get(("l1_ct",)))
        V.tensor_scalar(ctmp[:, :], xi, cw[1][:, et:et + 1], None, ALU.mult)
        V.drain()
        V.scalar_tensor_tensor(ctmp[:, 1:2048], xi[:, 0:2047], cw[0][:, et:et + 1],
                               ctmp[:, 1:2048], ALU.mult, ALU.add)
        V.drain()
        V.scalar_tensor_tensor(ctmp[:, 0:1], haloL[:, et:et + 1], cw[0][:, et:et + 1],
                               ctmp[:, 0:1], ALU.mult, ALU.add)
        V.drain()
        V.scalar_tensor_tensor(ctmp[:, 0:2047], xi[:, 1:2048], cw[2][:, et:et + 1],
                               ctmp[:, 0:2047], ALU.mult, ALU.add)
        V.drain()
        V.scalar_tensor_tensor(ctmp[:, 2047:2048], haloR[:, et:et + 1], cw[2][:, et:et + 1],
                               ctmp[:, 2047:2048], ALU.mult, ALU.add).inc(sV, 1)
        C.wait(sV)
        C.activation(xi, ctmp[:, :], AF.Silu, bias=cb[:, et:et + 1]).inc(sC, 1)
        buf[("l1_ct",)] = sC.guard()
    g_conv = sC.guard()

    V.memset(carry[:, :], 0.0).inc(sV, 1)
    for lc in range(4):
        for e2 in range(12):
            bi = e2 % 2
            S.wg(buf.get(("l1_w", bi)))
            if lc == 0 and e2 < 2:
                S.wg(mk.g_precast)
            S.dma_start(out=wbuf[bi][:, :], in_=dtw_bf[e2]).inc(sWB1[bi], 16)
            g_w = sWB1[bi].guard()
            pk = e2 % 4
            T.wg(g_conv)
            T.wg(g_w)
            T.wg(bank_free[pk])
            if lc > 0 or e2 > 0:
                T.wg(buf.get(("l1_p1w",)))
            for k in range(12):
                hnd = T.matmul(P[pk][:, :], wbuf[bi][:, 128 * k:128 * (k + 1)],
                               XI[:, 2048 * k + 512 * lc:2048 * k + 512 * (lc + 1)],
                               start=(k == 0), stop=(k == 11))
            hnd.inc(sT, 1)
            buf[("l1_w", bi)] = sT.guard()
            C.wait(sT)
            C.wg(buf.get(("l1_sc",)))
            C.activation(scf0[:, :], P[pk][:, :], AF.Relu, bias=dtb[:, e2:e2 + 1])
            C.drain()
            C.activation(scf1[:, :], P[pk][:, :], AF.Abs, bias=dtb[:, e2:e2 + 1]).inc(sC, 1)
            bank_free[pk] = sC.guard()
            C.drain()
            C.activation(scf1[:, :], scf1[:, :], AF.Exp, scale=-1.0)
            C.drain()
            C.activation(scf1[:, :], scf1[:, :], AF.Ln, bias=mk.onec[:, 0:1]).inc(sC, 1)
            V.wait(sC)
            V.tensor_tensor(scf0[:, :], scf0[:, :], scf1[:, :], ALU.add).inc(sV, 1)
            C.wait(sV)
            C.activation(scf1[:, :], scf0[:, :], AF.Exp, scale=Ab[:, e2:e2 + 1]).inc(sC, 1)
            V.wait(sC)
            xc = XI[:, 2048 * e2 + 512 * lc:2048 * e2 + 512 * (lc + 1)]
            sz = SZ[:, 2048 * e2 + 512 * lc:2048 * e2 + 512 * (lc + 1)]
            V.tensor_tensor(scf2[:, :], xc, scf0[:, :], ALU.mult)
            V.drain()
            V.tensor_tensor(scf2[:, :], scf2[:, :], scf1[:, :], ALU.mult)
            V.drain()
            V.tensor_tensor_scan(scf0[:, :], scf2[:, :], scf2[:, :],
                                 carry[:, e2:e2 + 1], ALU.add, ALU.bypass)
            V.drain()
            V.tensor_copy(carry[:, e2:e2 + 1], scf0[:, 511:512])
            V.drain()
            V.tensor_tensor(scf0[:, :], scf0[:, :], scf1[:, :], ALU.mult)
            V.drain()
            V.scalar_tensor_tensor(scf0[:, :], xc, Dpb[:, e2:e2 + 1], scf0[:, :],
                                   ALU.mult, ALU.add)
            V.drain()
            V.tensor_tensor(p1st[:, 512 * e2:512 * (e2 + 1)], scf0[:, :], sz, ALU.mult)
            V.drain()
            V.tensor_tensor(sz, scf1[:, :], sz, ALU.mult).inc(sV, 1)
            buf[("l1_sc",)] = sV.guard()
        hnd = None
        for e2 in range(12):
            hnd = V.tensor_copy(XI[:, 2048 * e2 + 512 * lc:2048 * e2 + 512 * (lc + 1)],
                                p1st[:, 512 * e2:512 * (e2 + 1)])
        hnd.inc(sV, 1)
        buf[("l1_p1w",)] = sV.guard()
    V.drain()
    V.tensor_copy(tot[:, :], carry[:, :]).inc(sV, 1)
    g_tot = sV.guard()

    T.wg(g_tot)
    T.wg(bank_free[6])
    T.wg(mk.g_idf)
    T.transpose(P[6][0:12, 0:128], tot[:, :], identf[:, :]).inc(sT, 1)
    V.wait(sT)
    V.tensor_copy(car_sb[:, :], P[6][0:12, 0:128]).inc(sV, 1)
    bank_free[6] = sV.guard()
    G.wait(sV)
    G.dma_start(out=car_in[:, :], in_=car_sb[:, :]).inc(sGx, 16)
    G.wg(sGx.guard())
    G.collective_compute("AllGather", ALU.bypass,
                         replica_groups=[[0, 1], [2, 3], [4, 5], [6, 7]],
                         ins=[car_in.ap()], outs=[car_out.ap()]).inc(sCC, 1)
    G.wait(sCC)
    G.dma_start(out=car_sb[:, :], in_=car_out[0:12, :]).inc(sGx, 16)
    g_carb = sGx.guard()
    T.wg(g_carb)
    T.wg(bank_free[6])
    T.wg(mk.g_idf)
    T.transpose(P[6][0:128, 0:12], car_sb[:, :], identf[0:12, 0:12]).inc(sT, 1)
    V.wait(sT)
    V.tensor_scalar(carry[:, :], P[6][:, 0:12], mh1[:, 0:1], None, ALU.mult).inc(sV, 1)
    bank_free[6] = sV.guard()
    g_carry = sV.guard()

    V.wg(g_carry)
    hnd = None
    for e2 in range(12):
        hnd = V.scalar_tensor_tensor(XI[:, 2048 * e2:2048 * (e2 + 1)],
                                     SZ[:, 2048 * e2:2048 * (e2 + 1)],
                                     carry[:, e2:e2 + 1],
                                     XI[:, 2048 * e2:2048 * (e2 + 1)],
                                     ALU.mult, ALU.add)
    hnd.inc(sV, 1)
    g_y = sV.guard()

    G.wg(g_y)
    G.dma_start(out=WOUT[:, :], in_=woutT_d).inc(sWOg, 16)
    g_wout = sWOg.guard()
    for lt in range(16):
        for n in range(2):
            pk = n + 2 * (lt % 2)
            T.wg(g_y)
            T.wg(g_wout)
            T.wg(bank_free[pk])
            for k in range(12):
                hnd = T.matmul(P[pk][:, :], XI[:, 2048 * k + 128 * lt:2048 * k + 128 * (lt + 1)],
                               WOUT[:, 1024 * k + 512 * n:1024 * k + 512 * (n + 1)],
                               start=(k == 0), stop=(k == 11))
            hnd.inc(sT, 1)
        V.wait(sT)
        for n in range(2):
            pk = n + 2 * (lt % 2)
            cs = 1024 * lt + 512 * n
            hnd = V.tensor_tensor(R[:, cs:cs + 512], P[pk][:, :], R[:, cs:cs + 512], ALU.add)
        hnd.inc(sV, 1)
        for n in range(2):
            bank_free[n + 2 * (lt % 2)] = sV.guard()



def emit_l1ffn(mk, R, P, PB, AR, identb, onesb,
               w1_sl_d, w3_sl_d, w2T_d, routerT_d, ebias_d):
    nc = mk.nc
    S, G, V, C, T = mk.S, mk.G, mk.V, mk.C, mk.T
    sems = mk.sems
    sS, sG, sT, sV, sC = (sems[k] for k in ("sS", "sG", "sT", "sV", "sC"))
    stat = mk.stat
    bank_free = mk.bank_free
    buf = mk.buf

    XT = mk.at("XT", [128, 16384], BF16, AR)
    GT = mk.at("GT", [128, 16384], BF16, AR + 32768)
    W2 = mk.at("W2", [128, 16384], BF16, AR + 65536)

    mk.flex_off = mk.FLEX0
    xn = mk.flex("mo_xn", [128, 1024], BF16)
    sqd = mk.flex("mo_sqd", [128, 1024], BF16)
    rtw = mk.flex("mo_rtw", [128, 64], BF16)
    ebias = mk.flex("mo_eb", [1, 8], BF16)
    lgt = mk.flex("mo_lgt", [128, 8], F32)
    eq = mk.flex("mo_eq", [128, 8], F32)
    expv = mk.flex("mo_expv", [128, 8], F32)
    comb = mk.flex("mo_comb", [128, 16 * 10], F32)
    w1b = [mk.flex(f"mo_w1{i}", [128, 1024], BF16) for i in range(2)]
    w3b = [mk.flex(f"mo_w3{i}", [128, 1024], BF16) for i in range(2)]
    sil = mk.flex("mo_sil", [128, 512], F32)
    assert mk.flex_off + SBUF_BASE <= SBUF_LIMIT

    sWL3 = mk.dsem()
    sW2 = mk.dsem()
    sW13 = [mk.dsem(), mk.dsem()]
    G.dma_start(out=rtw[:, :], in_=routerT_d).inc(sWL3, 16)
    G.dma_start(out=ebias[:, :], in_=ebias_d).inc(sWL3, 16)
    g_rt = sWL3.guard()

    for lt in range(16):
        src = R[:, 1024 * lt:1024 * (lt + 1)]
        g_st = mk.ln_stats(src, sqd[:, :])
        C.wg(buf.get(("mo_xn",)))
        g_xn = mk.ln_norm(src, xn[:, :], g_st)
        g_xtl = mk.tset(xn, lambda i, l_=lt: XT[:, 2048 * i + 128 * l_:2048 * i + 128 * (l_ + 1)],
                        8, g_xn)
        buf[("mo_xn",)] = sT.guard()
        pk = 6
        T.wg(g_xtl)
        T.wg(bank_free[pk])
        T.wg(g_rt)
        for k in range(8):
            T.matmul(P[pk][:, 0:8], XT[:, 2048 * k + 128 * lt:2048 * k + 128 * (lt + 1)],
                     rtw[:, 8 * k:8 * (k + 1)], start=(k == 0), stop=False)
        T.matmul(P[pk][:, 0:8], onesb[0:1, :], ebias[0:1, :],
                 start=False, stop=True).inc(sT, 1)
        V.wait(sT)
        V.tensor_copy(lgt[:, :], P[pk][:, 0:8]).inc(sV, 1)
        bank_free[pk] = sV.guard()
        V.drain()
        V.reduce_max(stat[:, 30:31], lgt[:, :], axis=AX.X)
        V.drain()
        V.tensor_scalar(eq[:, :], lgt[:, :], stat[:, 30:31], None, ALU.is_ge)
        V.drain()
        V.scalar_tensor_tensor(eq[:, :], eq[:, :], NEG, lgt[:, :], ALU.mult, ALU.add)
        V.drain()
        V.reduce_max(stat[:, 31:32], eq[:, :], axis=AX.X)
        V.drain()
        V.tensor_scalar(stat[:, 33:34], stat[:, 30:31], -1.0, None, ALU.mult).inc(sV, 1)
        C.wait(sV)
        C.activation(stat[:, 32:33], stat[:, 31:32], AF.Exp, bias=stat[:, 33:34]).inc(sC, 1)
        V.wait(sC)
        V.tensor_scalar(stat[:, 32:33], stat[:, 32:33], 1.0, None, ALU.add)
        V.drain()
        V.reciprocal(stat[:, 34:35], stat[:, 32:33]).inc(sV, 1)
        C.wait(sV)
        C.activation(expv[:, :], lgt[:, :], AF.Exp, bias=stat[:, 33:34]).inc(sC, 1)
        V.wait(sC)
        V.tensor_scalar(eq[:, :], lgt[:, :], stat[:, 31:32], None, ALU.is_ge)
        V.drain()
        V.tensor_scalar(expv[:, :], expv[:, :], stat[:, 34:35], None, ALU.mult)
        V.drain()
        V.tensor_tensor(comb[:, 10 * lt:10 * lt + 8], expv[:, :], eq[:, :], ALU.mult)
        V.drain()
        V.memset(comb[:, 10 * lt + 8:10 * lt + 10], 1.0).inc(sV, 1)
    g_xt = sV.guard()

    for e in range(NEXP):
        g_w2 = None
        for hh in range(2):
            for ht8 in range(8):
                ht = 8 * hh + ht8
                bi = ht8 % 2
                G.wg(buf.get(("mo_w1", bi)))
                G.dma_start(out=w1b[bi][:, :], in_=w1_sl_d[e, ht]).inc(sW13[bi], 16)
                G.dma_start(out=w3b[bi][:, :], in_=w3_sl_d[e, ht]).inc(sW13[bi], 16)
                g_w = sW13[bi].guard()
                if hh == 0 and ht8 == 0:
                    G.wg(buf.get(("mo_w2",)))
                    G.dma_start(out=W2[:, :], in_=w2T_d[e]).inc(sW2, 16)
                    g_w2 = sW2.guard()
                for lch in range(4):
                    T.wg(g_xt)
                    T.wg(g_w)
                    T.wg(bank_free[lch])
                    for k in range(8):
                        hnd = T.matmul(P[lch][:, :], w1b[bi][:, 128 * k:128 * (k + 1)],
                                       XT[:, 2048 * k + 512 * lch:2048 * k + 512 * (lch + 1)],
                                       start=(k == 0), stop=(k == 7))
                    hnd.inc(sT, 1)
                for lch in range(4):
                    pk3 = 4 + (lch % 2)
                    T.wg(bank_free[pk3])
                    for k in range(8):
                        hnd = T.matmul(P[pk3][:, :], w3b[bi][:, 128 * k:128 * (k + 1)],
                                       XT[:, 2048 * k + 512 * lch:2048 * k + 512 * (lch + 1)],
                                       start=(k == 0), stop=(k == 7))
                    hnd.inc(sT, 1)
                    if lch == 3:
                        buf[("mo_w1", bi)] = sT.guard()
                    C.wait(sT)
                    C.wg(buf.get(("mo_sil",)))
                    C.activation(sil[:, :], P[lch][:, :], AF.Silu).inc(sC, 1)
                    V.wait(sC)
                    V.tensor_tensor(GT[:, 2048 * ht8 + 512 * lch:2048 * ht8 + 512 * (lch + 1)],
                                    P[pk3][:, :], sil[:, :], ALU.mult).inc(sV, 1)
                    buf[("mo_sil",)] = sV.guard()
                    bank_free[lch] = sV.guard()
                    bank_free[pk3] = sV.guard()
            g_gt = sV.guard()
            for lt in range(16):
                pk0 = 2 * (lt % 2)
                for n in range(2):
                    pk = pk0 + n
                    T.wg(g_gt)
                    T.wg(g_w2)
                    T.wg(bank_free[pk])
                    for k in range(8):
                        kk = 8 * hh + k
                        hnd = T.matmul(P[pk][:, :],
                                       GT[:, 2048 * k + 128 * lt:2048 * k + 128 * (lt + 1)],
                                       W2[:, 1024 * kk + 512 * n:1024 * kk + 512 * (n + 1)],
                                       start=(k == 0), stop=(k == 7))
                    hnd.inc(sT, 1)
                if lt == 15:
                    buf[("mo_w2",)] = sT.guard()
                V.wait(sT)
                for n in range(2):
                    cs = 1024 * lt + 512 * n
                    hnd = V.scalar_tensor_tensor(R[:, cs:cs + 512], P[pk0 + n][:, :],
                                                 comb[:, 10 * lt + e:10 * lt + e + 1],
                                                 R[:, cs:cs + 512], ALU.mult, ALU.add)
                hnd.inc(sV, 1)
                for n in range(2):
                    bank_free[pk0 + n] = sV.guard()


_NC_CACHE = {}


def _get_nc(stages=4):
    if stages not in _NC_CACHE:
        _NC_CACHE[stages] = build_program(stages)
    return _NC_CACHE[stages]


def kernel(**inputs):
    per = _host_prep(inputs)
    nc = _get_nc(4)
    res = run_bass_kernel_spmd(nc, per, core_ids=list(range(8)))
    out = np.empty((B, T, D), np.float32)
    for core in range(8):
        b, h = core // 2, core % 2
        out[b, 2048 * h:2048 * h + 2048] = res.results[core]["out"]
    return out


# version 18
# speedup vs baseline: 1.0648x; 1.0648x over previous
import math
import numpy as np
import ml_dtypes
import concourse.bass as bass
import concourse.mybir as mybir
from concourse.bass_utils import run_bass_kernel_spmd

F32 = mybir.dt.float32
BF16 = mybir.dt.bfloat16
AF = mybir.ActivationFunctionType
ALU = mybir.AluOpType
AX = mybir.AxisListType

MAGIC = 12582912.0
NEG = -1e30
BF = ml_dtypes.bfloat16

B, T, D = 4, 4096, 1024
NLT = 16
H = 16
NEXP = 10
ENGS = ("sync", "gpsimd", "vector", "scalar", "tensor")
SBUF_BASE = 16512
SBUF_LIMIT = 229344



def _ktile(wT):
    K, N = wT.shape
    assert K % 128 == 0
    return np.ascontiguousarray(
        wT.reshape(K // 128, 128, N).transpose(1, 0, 2).reshape(128, -1))


def _sinusoidal(n, d=64):
    pos = np.arange(n, dtype=np.float32)[:, None]
    div = np.exp(np.arange(0, d, 2, dtype=np.float32) * (-math.log(10000.0) / d))
    ang = pos * div
    return np.stack([np.sin(ang), np.cos(ang)], axis=-1).reshape(n, d).astype(np.float32)


_S316 = _sinusoidal(316, 64)


def _rot_table(positions):
    S = _S316[positions]
    n = S.shape[0]
    M = np.empty((n, 64), np.float32)
    M[:, 0::2] = -S[:, 32:]
    M[:, 1::2] = S[:, :32]
    return np.tile(M, (1, H)).astype(BF)


def _host_prep(inp):
    g = {}
    wat = np.ascontiguousarray(np.asarray(inp["L0_attn_w_attn"], np.float32).T)
    g["wqT"] = _ktile(wat[:, :1024])
    g["wkvT"] = _ktile(wat[:, 1024:])
    g["bq"] = np.asarray(inp["L0_attn_b_attn"], np.float32)[None, :1024].copy()
    g["bkv"] = np.asarray(inp["L0_attn_b_attn"], np.float32)[None, 1024:].copy()
    g["wprojT"] = _ktile(np.ascontiguousarray(np.asarray(inp["L0_attn_w_proj"], np.float32).T))
    g["bproj"] = np.asarray(inp["L0_attn_b_proj"], np.float32)[None, :].copy()
    g["wassignT"] = _ktile(np.ascontiguousarray(np.asarray(inp["L0_attn_w_assign"], np.float32).T))
    g["wgF"] = _ktile(np.ascontiguousarray(np.asarray(inp["L0_ffn_w_gate"], np.float32).T))
    g["wuF"] = _ktile(np.ascontiguousarray(np.asarray(inp["L0_ffn_w_up"], np.float32).T))
    g["wdF"] = _ktile(np.ascontiguousarray(np.asarray(inp["L0_ffn_w_down"], np.float32).T))
    winT = np.ascontiguousarray(np.asarray(inp["L1_attn_W_in"], np.float32).T)
    g["winT_sl"] = np.ascontiguousarray(
        _ktile(winT).reshape(128, 8, 24, 128).transpose(2, 0, 1, 3).reshape(24, 128, 1024))
    dtwT = np.ascontiguousarray(np.asarray(inp["L1_attn_dt_w"], np.float32).T)
    g["dtwT_sl"] = np.ascontiguousarray(
        _ktile(dtwT).reshape(128, 12, 12, 128).transpose(2, 0, 1, 3).reshape(12, 128, 1536))
    g["woutT"] = _ktile(np.ascontiguousarray(np.asarray(inp["L1_attn_W_out"], np.float32).T))
    cw = np.asarray(inp["L1_attn_conv_w"], np.float32)[:, 0, :]
    for k in range(3):
        g[f"convw{k}"] = np.ascontiguousarray(cw[:, k].reshape(12, 128).T)
    g["convb"] = np.ascontiguousarray(np.asarray(inp["L1_attn_conv_b"], np.float32).reshape(12, 128).T)
    g["dtb"] = np.ascontiguousarray(np.asarray(inp["L1_attn_dt_b"], np.float32).reshape(12, 128).T)
    g["A_"] = np.ascontiguousarray(np.asarray(inp["L1_attn_A"], np.float32).reshape(12, 128).T)
    g["Dp"] = np.ascontiguousarray(np.asarray(inp["L1_attn_Dp"], np.float32).reshape(12, 128).T)
    w1c = np.concatenate([np.asarray(inp["L1_ffn_w1"], np.float32),
                          np.asarray(inp["L1_ffn_sw1"], np.float32)], 0)
    w3c = np.concatenate([np.asarray(inp["L1_ffn_w3"], np.float32),
                          np.asarray(inp["L1_ffn_sw3"], np.float32)], 0)
    w2c = np.concatenate([np.asarray(inp["L1_ffn_w2"], np.float32),
                          np.asarray(inp["L1_ffn_sw2"], np.float32)], 0)

    def _sl16(w):
        kt = _ktile(np.ascontiguousarray(w.T))
        return np.ascontiguousarray(
            kt.reshape(128, 8, 16, 128).transpose(2, 0, 1, 3).reshape(16, 128, 1024))
    g["w1_sl"] = np.ascontiguousarray(np.stack([_sl16(w1c[e]) for e in range(NEXP)]))
    g["w3_sl"] = np.ascontiguousarray(np.stack([_sl16(w3c[e]) for e in range(NEXP)]))
    g["w2T"] = np.ascontiguousarray(
        np.stack([_ktile(np.ascontiguousarray(w2c[e].T)) for e in range(NEXP)]))
    g["routerT"] = _ktile(np.ascontiguousarray(np.asarray(inp["L1_ffn_router_w"], np.float32).T))
    g["ebias"] = np.asarray(inp["L1_ffn_expert_bias"], np.float32)[None, :].copy()

    g["identf"] = np.eye(128, dtype=np.float32)
    g["identb"] = np.eye(128, dtype=np.float32).astype(BF)
    g["mq_tab"] = np.ascontiguousarray(_rot_table(np.arange(256)).reshape(2, 128, 1024))
    g["mc_tab"] = np.ascontiguousarray(_rot_table(np.arange(64)))

    x0 = np.asarray(inp["L0_attn_x"], dtype=np.float32)
    per = []
    for core in range(8):
        b, h = core // 2, core % 2
        own = x0[b, 2048 * h:2048 * h + 2048]
        oth = x0[b, 2048 * (1 - h):2048 * (1 - h) + 2048]
        d = dict(g)
        d["xseq"] = np.ascontiguousarray(
            np.concatenate([own, oth], 0).reshape(32, 128, 1024))
        m64 = np.zeros((32, 128, 64), np.float32)
        for j in range(32):
            ci = 16 * h + j if j < 16 else 16 * (1 - h) + (j - 16)
            m64[j, :64, 2 * ci] = 1.0
            m64[j, 64:, 2 * ci + 1] = 1.0
        d["mask64"] = m64.astype(BF)
        rows = []
        for j in range(16):
            g_abs = 8 * h + j // 2
            pos = 4 * g_abs + (j % 2) * 128 + np.arange(128)
            rows.append(_rot_table(pos))
        d["mkl_tab"] = np.ascontiguousarray(np.stack(rows))
        am = np.zeros((8, 2, 128, 316), np.float32)
        for gl in range(8):
            g_abs = 8 * h + gl
            am[gl, :, :, :60] = np.where(
                np.arange(60)[None, None, :] < 4 * g_abs, 0.0, NEG)
            for qc in range(2):
                qpos = 128 * qc + np.arange(128)
                am[gl, qc, :, 60:] = np.where(
                    np.arange(256)[None, :] <= qpos[:, None], 0.0, NEG)
        d["mask_att"] = am.astype(BF)
        d["mh1"] = np.full((128, 1), float(h == 1), np.float32)
        d["mh0"] = np.full((128, 1), float(h == 0), np.float32)
        per.append(d)
    return per



class Sem:
    __slots__ = ("h", "count")

    def __init__(self, h):
        self.h = h
        self.count = 0

    def guard(self):
        return (self, self.count)


class _Inc:
    __slots__ = ("item",)

    def __init__(self, item):
        self.item = item

    def inc(self, sem, n=1):
        self.item[3].append((sem, n))
        sem.count += n
        return self


class Rec:
    def __init__(self, prog, eng):
        self._p = prog
        self._e = eng

    def __getattr__(self, op):
        def f(*args, **kw):
            item = [op, args, kw, []]
            self._p[self._e].append(item)
            return _Inc(item)
        return f

    def wait(self, sem, val=None):
        if val is None:
            val = sem.count
        assert val <= sem.count, "wait on future semaphore value"
        if val > 0:
            self._p[self._e].append(["__wait__", (sem, val), {}, []])

    def wg(self, g):
        if g is not None:
            self.wait(g[0], g[1])


def _replay(prog, eng_name, eng_obj):
    for op, args, kw, incs in prog[eng_name]:
        if op == "__wait__":
            eng_obj.wait_ge(args[0].h, args[1])
            continue
        inst = getattr(eng_obj, op)(*args, **kw)
        for sem, n in incs:
            inst.then_inc(sem.h, n)



class MK:
    def __init__(self, nc, stages):
        self.nc = nc
        self.stages = stages
        self.prog = {e: [] for e in ENGS}
        self.S = Rec(self.prog, "sync")
        self.G = Rec(self.prog, "gpsimd")
        self.V = Rec(self.prog, "vector")
        self.C = Rec(self.prog, "scalar")
        self.T = Rec(self.prog, "tensor")
        self.recs = {"sync": self.S, "gpsimd": self.G, "vector": self.V,
                     "scalar": self.C, "tensor": self.T}
        self.flex_off = 163840
        self.bank_free = [None] * 8
        self.buf = {}

    def at(self, name, shape, dtype, off):
        off = off + SBUF_BASE
        end = off + int(np.prod(shape[1:])) * mybir.dt.size(dtype)
        assert end <= SBUF_LIMIT, f"{name}: sbuf overflow {end}"
        return self.nc.alloc_sbuf_tensor_at(name, list(shape), dtype, offset=off)

    def flex(self, name, shape, dtype):
        off = (self.flex_off + 31) // 32 * 32
        t = self.at(name, shape, dtype, off)
        self.flex_off = off + int(np.prod(shape[1:])) * mybir.dt.size(dtype)
        return t

    FLEX_LIMIT = SBUF_LIMIT - SBUF_BASE

    def barrier(self):
        for e in ENGS:
            self.recs[e].drain().inc(self.bsem[e], 1)
        tgt = {e: self.bsem[e].count for e in ENGS}
        for e in ENGS:
            for o in ENGS:
                if o != e:
                    self.recs[e].wait(self.bsem[o], tgt[o])
        self.bank_free = [None] * 8

    def ln_stats(self, src_ap, sq_dump, b=0):
        V, C, st = self.V, self.C, self.stat
        sV, sC = self.sems["sV"], self.sems["sC"]
        V.reduce_sum(st[:, b:b + 1], src_ap, axis=AX.X)
        V.drain()
        V.tensor_scalar(st[:, b + 8:b + 9], st[:, b:b + 1], 1.0 / 1024, None,
                        ALU.mult).inc(sV, 1)
        C.activation(sq_dump, src_ap, AF.Square, accum_out=st[:, b + 1:b + 2]).inc(sC, 1)
        V.wait(sC)
        V.drain()
        V.tensor_scalar(st[:, b + 2:b + 3], st[:, b + 1:b + 2], 1.0 / 1024, None, ALU.mult)
        V.drain()
        V.tensor_tensor(st[:, b + 3:b + 4], st[:, b + 8:b + 9], st[:, b + 8:b + 9], ALU.mult)
        V.drain()
        V.tensor_tensor(st[:, b + 2:b + 3], st[:, b + 2:b + 3], st[:, b + 3:b + 4],
                        ALU.subtract).inc(sV, 1)
        C.wait(sV)
        C.activation(st[:, b + 4:b + 5], st[:, b + 2:b + 3], AF.Ln,
                     bias=self.epsc[:, 0:1])
        C.drain()
        C.activation(st[:, b + 6:b + 7], st[:, b + 4:b + 5], AF.Exp,
                     scale=-0.5).inc(sC, 1)
        V.wait(sC)
        V.scalar_tensor_tensor(st[:, b + 5:b + 6], st[:, b + 8:b + 9], -1.0,
                               st[:, b + 6:b + 7], ALU.mult, ALU.mult).inc(sV, 1)
        return sV.guard()

    def ln_norm(self, src_ap, out_ap, g_stats, scale=None, bias=None, b=0):
        C, st = self.C, self.stat
        sC = self.sems["sC"]
        C.wg(g_stats)
        C.activation(out_ap, src_ap, AF.Identity,
                     bias=st[:, b + 5:b + 6] if bias is None else bias,
                     scale=st[:, b + 6:b + 7] if scale is None else scale).inc(sC, 1)
        return sC.guard()

    def tset(self, src, dst_fn, n, g_src, g_dst_free=None, src_col0=0):
        T, V = self.T, self.V
        sT, sV = self.sems["sT"], self.sems["sV"]
        PB, identb = self.PB, self.identb
        for i0 in range(0, n, 8):
            cnt = min(8, n - i0)
            T.wg(g_src)
            T.wg(getattr(self, "g_idb", None))
            T.wg(self.bank_free[7])
            hnd = None
            for i in range(cnt):
                hnd = T.transpose(PB[:, 128 * i:128 * (i + 1)],
                                  src[:, src_col0 + 128 * (i0 + i):src_col0 + 128 * (i0 + i + 1)],
                                  identb[:, :])
            hnd.inc(sT, 1)
            V.wait(sT)
            V.wg(g_dst_free)
            h2 = None
            for i in range(cnt):
                h2 = V.tensor_copy(dst_fn(i0 + i), PB[:, 128 * i:128 * (i + 1)])
            h2.inc(sV, 1)
            self.bank_free[7] = sV.guard()
        return sV.guard()


def build_program(stages=4):
    nc = bass.Bass("TRN2", target_bir_lowering=False)
    mk = MK(nc, stages)

    def din(name, shape, dt=F32):
        return nc.dram_tensor(name, list(shape), dt, kind="ExternalInput").ap()

    xseq = din("xseq", (32, 128, 1024))
    wqT_d = din("wqT", (128, 8 * 1024))
    wkvT_d = din("wkvT", (128, 8 * 2048))
    bq_d = din("bq", (1, 1024))
    bkv_d = din("bkv", (1, 2048))
    wprojT_d = din("wprojT", (128, 8 * 1024))
    bproj_d = din("bproj", (1, 1024))
    wassignT_d = din("wassignT", (128, 8 * 64))
    wgF_d = din("wgF", (128, 8 * 4096))
    wuF_d = din("wuF", (128, 8 * 4096))
    wdF_d = din("wdF", (128, 32 * 1024))
    winT_sl_d = din("winT_sl", (24, 128, 1024))
    dtwT_sl_d = din("dtwT_sl", (12, 128, 1536))
    woutT_d = din("woutT", (128, 12 * 1024))
    convw_d = [din(f"convw{k}", (128, 12)) for k in range(3)]
    convb_d = din("convb", (128, 12))
    dtb_d = din("dtb", (128, 12))
    A_d = din("A_", (128, 12))
    Dp_d = din("Dp", (128, 12))
    w1_sl_d = din("w1_sl", (NEXP, 16, 128, 1024))
    w3_sl_d = din("w3_sl", (NEXP, 16, 128, 1024))
    w2T_d = din("w2T", (NEXP, 128, 16 * 1024))
    routerT_d = din("routerT", (128, 8 * 8))
    ebias_d = din("ebias", (1, 8))
    identf_d = din("identf", (128, 128))
    identb_d = din("identb", (128, 128), BF16)
    mq_d = din("mq_tab", (2, 128, 1024), BF16)
    mc_d = din("mc_tab", (64, 1024), BF16)
    mask64_d = din("mask64", (32, 128, 64), BF16)
    mkl_d = din("mkl_tab", (16, 128, 1024), BF16)
    matt_d = din("mask_att", (8, 2, 128, 316), BF16)
    mh1_d = din("mh1", (128, 1))
    mh0_d = din("mh0", (128, 1))
    out_d = nc.dram_tensor("out", [2048, 1024], F32, kind="ExternalOutput").ap()

    wgq_s = nc.dram_tensor("wgq_s", [128, 8 * 4096], BF16).ap()
    win_bf = nc.dram_tensor("win_bf", [24, 128, 1024], BF16).ap()
    dtw_bf = nc.dram_tensor("dtw_bf", [12, 128, 1536], BF16).ap()
    wuq_s = nc.dram_tensor("wuq_s", [128, 8 * 4096], BF16).ap()
    wdq_s = nc.dram_tensor("wdq_s", [128, 32 * 1024], BF16).ap()
    ckp_in = nc.dram_tensor("ckp_in", [64, 2052], F32)
    ckp_out = nc.dram_tensor("ckp_out", [64, 2052], F32)
    edge_in = nc.dram_tensor("edge_in", [24, 128], F32)
    edge_out = nc.dram_tensor("edge_out", [48, 128], F32)
    car_in = nc.dram_tensor("car_in", [12, 128], F32)
    car_out = nc.dram_tensor("car_out", [24, 128], F32)

    from contextlib import ExitStack
    ctx = ExitStack()
    P = [ctx.enter_context(nc.psum_tensor(f"p{i}", [128, 512], F32)) for i in range(7)]
    PB = ctx.enter_context(nc.psum_tensor("pb", [128, 1024], BF16))
    sems = {}
    for sname in ("sS", "sG", "sT", "sV", "sC", "sCC",
                  "bS", "bG", "bT", "bV", "bC"):
        sems[sname] = Sem(ctx.enter_context(nc.semaphore(sname)))
    _pool = [Sem(ctx.enter_context(nc.semaphore(f"dq{i}"))) for i in range(34)]
    _pi = [0]

    def dsem():
        _pi[0] += 1
        return _pool[_pi[0] - 1]
    mk.dsem = dsem
    mk.sems = sems
    sS, sG, sT, sV, sC, sCC = (sems[k] for k in ("sS", "sG", "sT", "sV", "sC", "sCC"))
    mk.bsem = {"sync": sems["bS"], "gpsimd": sems["bG"], "tensor": sems["bT"],
               "vector": sems["bV"], "scalar": sems["bC"]}
    mk.PB = PB
    S, G, V, C, T = mk.S, mk.G, mk.V, mk.C, mk.T
    bank_free = mk.bank_free
    buf = mk.buf

    R = mk.at("R", [128, 16384], F32, 0)
    AR = 65536
    identf = mk.flex("identf", [128, 128], F32)
    identb = mk.flex("identb", [128, 128], BF16)
    mk.identb = identb
    onesb = mk.flex("onesb", [128, 128], BF16)
    stat = mk.flex("stat", [128, 64], F32)
    mk.stat = stat
    mu_t = mk.flex("mu", [128, 16], F32)
    rs_t = mk.flex("rstd", [128, 16], F32)
    mh1 = mk.flex("mh1", [128, 1], F32)
    mh0 = mk.flex("mh0", [128, 1], F32)
    epsc = mk.flex("epsc", [128, 1], F32)
    onec = mk.flex("onec", [128, 1], F32)
    mk.epsc, mk.onec = epsc, onec
    FLEX0 = mk.flex_off
    mk.FLEX0 = FLEX0

    sCst = mk.dsem()
    sR = mk.dsem()
    S.dma_start(out=identf[:, :], in_=identf_d).inc(sCst, 16)
    S.dma_start(out=mh1[:, :], in_=mh1_d).inc(sCst, 16)
    S.dma_start(out=mh0[:, :], in_=mh0_d).inc(sCst, 16)
    sIdb = mk.dsem()
    G.dma_start(out=identb[:, :], in_=identb_d).inc(sIdb, 16)
    g_idb = sIdb.guard()
    mk.g_idb = g_idb
    V.memset(onesb[:, :], 1.0)
    V.memset(epsc[:, :], 1e-5)
    V.memset(onec[:, :], 1.0).inc(sV, 1)
    g_ones = sV.guard()
    S.dma_start(out=R[:, :].rearrange("p (a d) -> p a d", a=16),
                in_=xseq[0:16].transpose([1, 0, 2])).inc(sR, 16)
    g_R = sR.guard()
    g_small = sCst.guard()
    mk.g_idf = g_small

    KRT = mk.at("KRT", [128, 16384], BF16, AR)
    Vn = mk.at("Vn", [128, 16384], BF16, AR + 32768)
    WKV = mk.at("WKV", [128, 16384], BF16, AR + 65536)
    WQ = mk.at("WQ", [128, 8192], BF16, AR + 65536)
    WPJ = mk.at("WPJ", [128, 8192], BF16, AR + 65536 + 16384)

    mk.flex_off = FLEX0
    xn_b = [mk.flex(f"xn{i}", [128, 1024], BF16) for i in range(2)]
    xnt_b = [mk.flex(f"xnt{i}", [128, 1024], BF16) for i in range(2)]
    qbuf = mk.flex("qbuf", [128, 1024], BF16)
    sqd = mk.flex("sqd", [128, 1024], BF16)
    ckcv = mk.flex("ckcv", [64, 2048], BF16)
    kcrT = mk.flex("kcrT", [128, 512], BF16)
    mc_b = mk.flex("mcb", [64, 1024], BF16)
    bias_t = mk.flex("biast", [1, 2048], BF16)
    bkv_b = bias_t
    PH0 = mk.flex_off
    ckp = mk.flex("ckp", [64, 2052], F32)
    kvev = mk.flex("kvev", [128, 2064], BF16)
    mkl_b = [mk.flex(f"mkl{i}", [128, 1024], BF16) for i in range(2)]
    m64_b = [mk.flex(f"m64{i}", [128, 64], BF16) for i in range(2)]
    ew64 = mk.flex("ew64", [128, 64], BF16)
    was_b = mk.flex("wasb", [128, 512], BF16)
    asg_f = mk.flex("asgf", [128, 64], F32)
    assert mk.flex_off + SBUF_BASE <= SBUF_LIMIT, mk.flex_off
    mk.flex_off = PH0
    mq_b = mk.flex("mqb", [128, 2048], BF16)
    matt_b = [mk.flex("matt0", [128, 632], BF16)]
    qrt = [mk.flex(f"qrt{i}", [128, 2048], BF16) for i in range(2)]
    yt = mk.flex("yt", [128, 2048], BF16)
    ep_f = [mk.flex(f"epf{i}", [128, 316], F32) for i in range(2)]
    ep_b = [mk.flex(f"epb{i}", [128, 316], BF16) for i in range(3)]
    pt_b = [mk.flex(f"ptb{i}", [128, 384], BF16) for i in range(2)]
    assert mk.flex_off + SBUF_BASE <= SBUF_LIMIT, mk.flex_off
    bq_b = bias_t[0:1, 0:1024]
    bpj_b = bias_t[0:1, 1024:2048]

    if stages >= 1:
        sWL0 = mk.dsem()
        G.dma_start(out=WKV[:, :], in_=wkvT_d).inc(sWL0, 16)
        G.dma_start(out=bkv_b[:, :], in_=bkv_d).inc(sWL0, 16)
        G.dma_start(out=was_b[:, :], in_=wassignT_d).inc(sWL0, 16)
        G.dma_start(out=mc_b[:, :], in_=mc_d).inc(sWL0, 16)
        g_wkv = sWL0.guard()
        sPC = mk.dsem()
        G.dma_start(out=win_bf, in_=winT_sl_d).inc(sPC, 16)
        G.dma_start(out=dtw_bf, in_=dtwT_sl_d).inc(sPC, 16)
        mk.g_precast = sPC.guard()
        mk.win_bf, mk.dtw_bf = win_bf, dtw_bf
        sXC = [mk.dsem(), mk.dsem()]
        sMKL = [mk.dsem(), mk.dsem()]
        sM64 = [mk.dsem(), mk.dsem()]
        sMT = [mk.dsem(), mk.dsem()]
        sWQg = mk.dsem()

        def pre_chunk(j):
            bi = j % 2
            G.wg(buf.get(("m64", bi)))
            G.dma_start(out=m64_b[bi][:, :], in_=mask64_d[j]).inc(sM64[bi], 16)
            buf[("g_m64", bi)] = sM64[bi].guard()
            S.wg(buf.get(("mkl", bi)))
            S.dma_start(out=mkl_b[bi][:, :], in_=mkl_d[j]).inc(sMKL[bi], 16)
            buf[("g_mkl", bi)] = sMKL[bi].guard()
            src = R[:, 1024 * j:1024 * (j + 1)]
            lb = 16 + 10 * bi
            V.wg(g_R)
            C.wg(g_R)
            g_st = mk.ln_stats(src, sqd[:, :], b=lb)
            V.tensor_copy(mu_t[:, j:j + 1], stat[:, lb + 8:lb + 9])
            V.tensor_copy(rs_t[:, j:j + 1], stat[:, lb + 6:lb + 7]).inc(sV, 1)
            C.wg(buf.get(("xn", bi)))
            g_xn = mk.ln_norm(src, xn_b[bi][:, :], g_st, b=lb)
            buf[("g_xn", bi)] = g_xn

        def main_chunk(j):
            bi = j % 2
            g_xn = buf[("g_xn", bi)]
            g_xnt = mk.tset(xn_b[bi], lambda i, b_=bi: xnt_b[b_][:, 128 * i:128 * (i + 1)],
                            8, g_xn, g_dst_free=buf.get(("xnt", bi)))
            buf[("xn", bi)] = sT.guard()
            for n in range(4):
                pk = 5 + (n % 2)
                T.wg(g_xnt)
                T.wg(g_wkv)
                T.wg(bank_free[pk])
                if n == 0:
                    T.wg(buf.get(("kvev",)))
                for k in range(8):
                    T.matmul(P[pk][:, :], xnt_b[bi][:, 128 * k:128 * (k + 1)],
                             WKV[:, 2048 * k + 512 * n:2048 * k + 512 * (n + 1)],
                             start=(k == 0), stop=False)
                T.matmul(P[pk][:, :], onesb[0:1, :], bkv_b[0:1, 512 * n:512 * (n + 1)],
                         start=False, stop=True).inc(sT, 1)
                V.wait(sT)
                V.tensor_copy(kvev[:, 512 * n:512 * (n + 1)], P[pk][:, :]).inc(sV, 1)
                bank_free[pk] = sV.guard()
            if j == 0:
                V.memset(kvev[:, 2048:2064], 0.0)
                V.drain()
                V.memset(kvev[:, 2048:2049], 1.0).inc(sV, 1)
            g_kvev = sV.guard()
            T.wg(bank_free[6])
            for k in range(8):
                hnd = T.matmul(P[6][:, 0:64], xnt_b[bi][:, 128 * k:128 * (k + 1)],
                               was_b[:, 64 * k:64 * (k + 1)],
                               start=(k == 0), stop=(k == 7))
            hnd.inc(sT, 1)
            buf[("xnt", bi)] = sT.guard()
            V.wait(sT)
            V.tensor_copy(asg_f[:, :], P[6][:, 0:64]).inc(sV, 1)
            bank_free[6] = sV.guard()
            V.wg(buf.get(("g_m64", bi)))
            V.drain()
            V.tensor_tensor(asg_f[:, :], asg_f[:, :], m64_b[bi][:, :], ALU.mult)
            V.drain()
            V.reduce_sum(stat[:, 36:37], asg_f[:, :], axis=AX.X).inc(sV, 1)
            C.wait(sV)
            C.activation(stat[:, 37:38], stat[:, 36:37], AF.Exp).inc(sC, 1)
            V.wait(sC)
            V.tensor_scalar(ew64[:, :], m64_b[bi][:, :], stat[:, 37:38], None,
                            ALU.mult).inc(sV, 1)
            buf[("m64", bi)] = sV.guard()
            g_ew = sV.guard()
            T.wg(g_ew)
            T.wg(g_kvev)
            if j == 0:
                for pk in range(5):
                    T.wg(bank_free[pk])
            for n in range(4):
                T.matmul(P[n][0:64, :], ew64[:, :], kvev[:, 512 * n:512 * (n + 1)],
                         start=(j == 0), stop=(j == 15))
            hnd = T.matmul(P[4][0:64, 0:1], ew64[:, :], kvev[:, 2048:2049],
                           start=(j == 0), stop=(j == 15))
            hnd.inc(sT, 1)
            buf[("kvev",)] = sT.guard()
            V.tensor_copy(Vn[:, 1024 * j:1024 * (j + 1)], kvev[:, 1024:2048]).inc(sV, 1)
            V.wg(buf.get(("g_mkl", bi)))
            V.wg(buf.get(("qbuf",)))
            V.tensor_tensor(qbuf[:, 0:1024:2], kvev[:, 1:1024:2],
                            mkl_b[bi][:, 0:1024:2], ALU.mult)
            V.tensor_tensor(qbuf[:, 1:1024:2], kvev[:, 0:1024:2],
                            mkl_b[bi][:, 1:1024:2], ALU.mult).inc(sV, 1)
            buf[("mkl", bi)] = sV.guard()
            g_kr = sV.guard()
            mk.tset(qbuf, lambda i, j_=j: KRT[:, 2048 * i + 128 * j_:2048 * i + 128 * (j_ + 1)],
                    8, g_kr)
            buf[("qbuf",)] = sT.guard()

        for j in range(17):
            if j < 16:
                pre_chunk(j)
            if j >= 1:
                main_chunk(j - 1)

        g_sum = sT.guard()
        sCK = mk.dsem()
        V.wg(g_sum)
        for n in range(4):
            V.tensor_copy(ckp[0:64, 512 * n:512 * (n + 1)], P[n][0:64, :])
        V.tensor_copy(ckp[0:64, 2048:2049], P[4][0:64, 0:1])
        hnd = V.memset(ckp[0:64, 2049:2052], 0.0)
        hnd.inc(sV, 1)
        for pk in range(5):
            bank_free[pk] = sV.guard()
        G.wait(sV)
        G.dma_start(out=ckp_in.ap(), in_=ckp[0:64, :]).inc(sCK, 16)
        G.wg(sCK.guard())
        G.collective_compute("AllReduce", ALU.add,
                             replica_groups=[[0, 1], [2, 3], [4, 5], [6, 7]],
                             ins=[ckp_in.ap()], outs=[ckp_out.ap()]).inc(sCC, 1)
        G.wait(sCC)
        G.dma_start(out=ckp[0:64, :], in_=ckp_out.ap()).inc(sCK, 16)
        g_ckb = sCK.guard()
        V.wg(g_ckb)
        V.reciprocal(stat[0:64, 38:39], ckp[0:64, 2048:2049])
        V.drain()
        for n in range(4):
            hnd = V.tensor_scalar(ckcv[0:64, 512 * n:512 * (n + 1)], ckp[0:64, 512 * n:512 * (n + 1)],
                                  stat[0:64, 38:39], None, ALU.mult)
        hnd.inc(sV, 1)
        V.drain()
        V.wg(buf.get(("qbuf",)))
        V.tensor_tensor(qbuf[0:64, 0:1024:2], ckcv[0:64, 1:1024:2],
                        mc_b[0:64, 0:1024:2], ALU.mult)
        V.tensor_tensor(qbuf[0:64, 1:1024:2], ckcv[0:64, 0:1024:2],
                        mc_b[0:64, 1:1024:2], ALU.mult).inc(sV, 1)
        g_kcr = sV.guard()
        T.wg(g_kcr)
        T.wg(bank_free[7])
        for i in range(8):
            hnd = T.transpose(PB[:, 64 * i:64 * (i + 1)],
                              qbuf[0:64, 128 * i:128 * (i + 1)], identb[0:64, 0:64])
        hnd.inc(sT, 1)
        V.wait(sT)
        V.tensor_copy(kcrT[:, :], PB[:, 0:512]).inc(sV, 1)
        bank_free[7] = sV.guard()
        g_kcrT = sV.guard()

        g_ph0v = sV.guard()
        g_ph0c = sC.guard()
        G.wg(g_sum)
        G.wg(g_ph0v)
        G.wg(g_ph0c)
        G.dma_start(out=WQ[:, :], in_=wqT_d).inc(sWQg, 16)
        G.dma_start(out=WPJ[:, :], in_=wprojT_d).inc(sWQg, 16)
        G.dma_start(out=bias_t[0:1, 0:1024], in_=bq_d).inc(sWQg, 16)
        G.dma_start(out=bias_t[0:1, 1024:2048], in_=bproj_d).inc(sWQg, 16)
        G.dma_start(out=mq_b[:, :].rearrange("p (a d) -> p a d", a=2),
                    in_=mq_d.transpose([1, 0, 2])).inc(sWQg, 16)
        g_wq = sWQg.guard()

        def maskq_gl(gl):
            mbi = 0
            mbi = 0
            if gl < 2:
                S.wg(g_ph0v)
                S.wg(g_ph0c)
                S.wg(g_sum)
            S.wg(buf.get(("matt", mbi)))
            S.dma_start(out=matt_b[mbi][:, :].rearrange("p (a d) -> p a d", a=2),
                        in_=matt_d[gl].transpose([1, 0, 2])).inc(sMT[mbi], 16)
            mk.g_mask = sMT[mbi].guard()
            for qc in range(2):
                j = 2 * gl + qc
                src = R[:, 1024 * j:1024 * (j + 1)]
                V.wg(buf.get(("lnb",)))
                V.scalar_tensor_tensor(stat[:, 5:6], mu_t[:, j:j + 1], -1.0,
                                       rs_t[:, j:j + 1], ALU.mult, ALU.mult).inc(sV, 1)
                C.wait(sV)
                C.wg(buf.get(("xn", 0)))
                C.activation(xn_b[0][:, :], src, AF.Identity,
                             bias=stat[:, 5:6], scale=rs_t[:, j:j + 1]).inc(sC, 1)
                buf[("lnb",)] = sC.guard()
                g_xn = sC.guard()
                g_xnt = mk.tset(xn_b[0], lambda i: xnt_b[0][:, 128 * i:128 * (i + 1)],
                                8, g_xn, g_dst_free=buf.get(("xnt", 0)))
                buf[("xn", 0)] = sT.guard()
                for n in range(2):
                    T.wg(g_xnt)
                    T.wg(g_wq)
                    T.wg(bank_free[n])
                    for k in range(8):
                        T.matmul(P[n][:, :], xnt_b[0][:, 128 * k:128 * (k + 1)],
                                 WQ[:, 1024 * k + 512 * n:1024 * k + 512 * (n + 1)],
                                 start=(k == 0), stop=False)
                    hnd = T.matmul(P[n][:, :], onesb[0:1, :], bq_b[:, 512 * n:512 * (n + 1)],
                                   start=False, stop=True)
                hnd.inc(sT, 1)
                buf[("xnt", 0)] = sT.guard()
                V.wait(sT)
                V.wg(buf.get(("qbuf",)))
                V.tensor_copy(qbuf[:, 0:512], P[0][:, :])
                V.tensor_copy(qbuf[:, 512:1024], P[1][:, :]).inc(sV, 1)
                bank_free[0] = sV.guard()
                bank_free[1] = sV.guard()
                V.drain()
                V.wg(g_wq)
                V.wg(buf.get(("xn1",)))
                V.scalar_tensor_tensor(xn_b[1][:, 0:1024:2], qbuf[:, 1:1024:2], 0.125,
                                       mq_b[:, 1024 * qc:1024 * (qc + 1)][:, 0:1024:2],
                                       ALU.mult, ALU.mult)
                V.scalar_tensor_tensor(xn_b[1][:, 1:1024:2], qbuf[:, 0:1024:2], 0.125,
                                       mq_b[:, 1024 * qc:1024 * (qc + 1)][:, 1:1024:2],
                                       ALU.mult, ALU.mult).inc(sV, 1)
                buf[("qbuf",)] = sV.guard()
                g_qr = sV.guard()
                mk.tset(xn_b[1], lambda i, q_=qc: qrt[q_][:, 128 * i:128 * (i + 1)],
                        8, g_qr, g_dst_free=buf.get(("qrt", qc)))
                buf[("xn1",)] = sT.guard()
            mk.g_qrt = sV.guard()


        def pipe_gl(gl):
            mbi = 0
            NIT = 32
            g_sc = [None] * NIT
            g_m = [None] * NIT
            g_e = [None] * NIT
            g_n = [None] * NIT
            g_tr = [None] * NIT
            g_pc = [None] * NIT
            g_pv = [None] * NIT

            def it(i):
                qc, hh = i // 16, i % 16
                return qc, hh, hh // 2, 64 * (hh % 2), i % 6, i % 2, i % 3

            for tick in range(NIT + 6):
                i = tick - 6
                if 0 <= i < NIT:
                    qc, hh, dt_, po, pk6, b2, b3 = it(i)
                    C.wait(sT, g_pv[i][1])
                    C.copy(yt[po:po + 64, 256 * dt_ + 128 * qc:256 * dt_ + 128 * (qc + 1)],
                           P[pk6][po:po + 64, 384:512]).inc(sC, 1)
                    bank_free[pk6] = sC.guard()
                i = tick - 5
                if 0 <= i < NIT:
                    qc, hh, dt_, po, pk6, b2, b3 = it(i)
                    pbo = 384 * b2
                    C.wait(sT, g_tr[i][1])
                    C.wg(g_pv[i - 2] if i >= 2 else None)
                    C.copy(pt_b[b2][:, :], PB[:, pbo:pbo + 384]).inc(sC, 1)
                    g_pc[i] = sC.guard()
                    T.wg(g_pc[i])
                    T.matmul(P[pk6][po:po + 64, 384:512],
                             ckcv[0:60, 1024 + 64 * hh:1024 + 64 * (hh + 1)],
                             pt_b[b2][0:60, 0:128], start=True, stop=False)
                    T.matmul(P[pk6][po:po + 64, 384:512],
                             Vn[:, 2048 * gl + 64 * hh:2048 * gl + 64 * (hh + 1)],
                             pt_b[b2][:, 128:256], start=False, stop=False)
                    T.matmul(P[pk6][po:po + 64, 384:512],
                             Vn[:, 2048 * gl + 1024 + 64 * hh:2048 * gl + 1024 + 64 * (hh + 1)],
                             pt_b[b2][:, 256:384], start=False, stop=True).inc(sT, 1)
                    g_pv[i] = sT.guard()
                i = tick - 4
                if 0 <= i < NIT:
                    qc, hh, dt_, po, pk6, b2, b3 = it(i)
                    pbo = 384 * b2
                    T.wg(g_n[i])
                    T.wg(g_pc[i - 2] if i >= 2 else None)
                    T.transpose(PB[0:60, pbo:pbo + 128], ep_b[b3][0:128, 0:60], identb[:, :])
                    T.transpose(PB[0:128, pbo + 128:pbo + 256], ep_b[b3][:, 60:188],
                                identb[:, :])
                    T.transpose(PB[0:128, pbo + 256:pbo + 384], ep_b[b3][:, 188:316],
                                identb[:, :]).inc(sT, 1)
                    g_tr[i] = sT.guard()
                i = tick - 3
                if 0 <= i < NIT:
                    qc, hh, dt_, po, pk6, b2, b3 = it(i)
                    V.wg(g_e[i])
                    V.reciprocal(stat[:, 44 + b3:45 + b3], stat[:, 40 + b3:41 + b3])
                    V.drain()
                    V.tensor_scalar(ep_b[b3][:, :], ep_b[b3][:, :],
                                    stat[:, 44 + b3:45 + b3], None, ALU.mult).inc(sV, 1)
                    g_n[i] = sV.guard()
                i = tick - 2
                if 0 <= i < NIT:
                    qc, hh, dt_, po, pk6, b2, b3 = it(i)
                    C.wg(g_m[i])
                    C.wg(g_tr[i - 3] if i >= 3 else None)
                    C.activation(ep_b[b3][:, :], ep_f[b2][:, :], AF.Exp,
                                 accum_out=stat[:, 40 + b3:41 + b3]).inc(sC, 1)
                    g_e[i] = sC.guard()
                i = tick - 1
                if 0 <= i < NIT:
                    qc, hh, dt_, po, pk6, b2, b3 = it(i)
                    V.wg(g_sc[i])
                    V.wg(mk.g_mask)
                    V.wg(g_e[i - 2] if i >= 2 else None)
                    V.scalar_tensor_tensor(ep_f[b2][:, :], P[pk6][:, 0:316], 80.0,
                                           matt_b[mbi][:, 316 * qc:316 * (qc + 1)],
                                           ALU.min, ALU.add).inc(sV, 1)
                    g_m[i] = sV.guard()
                if tick < NIT:
                    qc, hh, dt_, po, pk6, b2, b3 = it(tick)
                    T.wg(mk.g_qrt)
                    if tick == 0:
                        T.wg(g_kcrT)
                        T.wg(buf.get(("yt",)))
                    T.wg(bank_free[pk6])
                    T.matmul(P[pk6][0:128, 0:60],
                             qrt[qc][po:po + 64, 128 * dt_:128 * (dt_ + 1)],
                             kcrT[po:po + 64, 64 * dt_:64 * dt_ + 60],
                             start=True, stop=True)
                    T.matmul(P[pk6][0:128, 60:316],
                             qrt[qc][po:po + 64, 128 * dt_:128 * (dt_ + 1)],
                             KRT[po:po + 64, 2048 * dt_ + 256 * gl:2048 * dt_ + 256 * (gl + 1)],
                             start=True, stop=True).inc(sT, 1)
                    g_sc[tick] = sT.guard()
            buf[("matt", mbi)] = sV.guard()
            buf[("qrt", 0)] = sT.guard()
            buf[("qrt", 1)] = sT.guard()
            mk.g_yt = sC.guard()

        def proj_gl(gl):
            for qc in range(2):
                j = 2 * gl + qc
                for n in range(2):
                    T.wg(mk.g_yt)
                    T.wg(g_wq)
                    T.wg(bank_free[n])
                    for k in range(8):
                        T.matmul(P[n][:, :], yt[:, 256 * k + 128 * qc:256 * k + 128 * (qc + 1)],
                                 WPJ[:, 1024 * k + 512 * n:1024 * k + 512 * (n + 1)],
                                 start=(k == 0), stop=False)
                    hnd = T.matmul(P[n][:, :], onesb[0:1, :], bpj_b[:, 512 * n:512 * (n + 1)],
                                   start=False, stop=True)
                hnd.inc(sT, 1)
                if qc == 1:
                    buf[("yt",)] = sT.guard()
                V.wait(sT)
                V.tensor_tensor(R[:, 1024 * j:1024 * j + 512], P[0][:, :],
                                R[:, 1024 * j:1024 * j + 512], ALU.add)
                V.tensor_tensor(R[:, 1024 * j + 512:1024 * (j + 1)], P[1][:, :],
                                R[:, 1024 * j + 512:1024 * (j + 1)], ALU.add).inc(sV, 1)
                bank_free[0] = sV.guard()
                bank_free[1] = sV.guard()

        maskq_gl(0)
        for gl in range(8):
            pipe_gl(gl)
            if gl + 1 < 8:
                maskq_gl(gl + 1)
            proj_gl(gl)
        mk.barrier()

    if stages >= 2:
        emit_l0ffn(mk, R, P, PB, AR, identb, onesb,
                   wgF_d, wuF_d, wdF_d, wgq_s, wuq_s, wdq_s, identf)
        mk.barrier()

    if stages >= 3:
        emit_l1attn(mk, R, P, PB, AR, identb, identf, winT_sl_d, dtwT_sl_d,
                    woutT_d, convw_d, convb_d, dtb_d, A_d, Dp_d,
                    edge_in, edge_out, car_in, car_out, mh1, mh0)
        mk.barrier()

    if stages >= 4:
        emit_l1ffn(mk, R, P, PB, AR, identb, onesb,
                   w1_sl_d, w3_sl_d, w2T_d, routerT_d, ebias_d)
        mk.barrier()

    S.wait(sV, sV.count)
    S.dma_start(out=out_d.rearrange("(a p) d -> a p d", p=128).transpose([1, 0, 2]),
                in_=R[:, :].rearrange("p (a d) -> p a d", a=16)).inc(sS, 16)
    S.wait(sS, sS.count)

    with nc.Block() as block:
        @block.sync
        def _(e):
            _replay(mk.prog, "sync", e)

        @block.gpsimd
        def _(e):
            _replay(mk.prog, "gpsimd", e)

        @block.vector
        def _(e):
            _replay(mk.prog, "vector", e)

        @block.scalar
        def _(e):
            _replay(mk.prog, "scalar", e)

        @block.tensor
        def _(e):
            _replay(mk.prog, "tensor", e)
    ctx.close()
    return nc



def emit_l0ffn(mk, R, P, PB, AR, identb, onesb,
               wgF_d, wuF_d, wdF_d, wgq_s, wuq_s, wdq_s, identf):
    nc = mk.nc
    S, G, V, C, T = mk.S, mk.G, mk.V, mk.C, mk.T
    sems = mk.sems
    sS, sG, sT, sV, sC = (sems[k] for k in ("sS", "sG", "sT", "sV", "sC"))
    stat = mk.stat
    bank_free = mk.bank_free
    buf = mk.buf

    WSF = [mk.at(f"wsf{i}", [128, 4096], F32, AR + 16384 * i) for i in range(2)]
    WQB = [mk.at(f"wqb{i}", [128, 4096], BF16, AR + 32768 + 8192 * i) for i in range(2)]
    XQT = mk.at("XQT", [128, 2048], BF16, AR)
    GS = mk.at("GS", [128, 8192], F32, AR + 4096)
    GQT = mk.at("GQT", [128, 8192], BF16, AR + 4096 + 32768)
    WGB = [mk.at(f"wgb{i}", [128, 4096], BF16, AR + 4096 + 49152 + 8192 * i)
           for i in range(2)]
    WUB = [mk.at(f"wub{i}", [128, 4096], BF16, AR + 4096 + 65536 + 8192 * i)
           for i in range(2)]
    WDB = WGB

    mk.flex_off = mk.FLEX0
    xf = mk.flex("l0f_xf", [128, 1024], F32)
    xq = mk.flex("l0f_xq", [128, 1024], BF16)
    gq = mk.flex("l0f_gq", [128, 4096], BF16)
    sil = mk.flex("l0f_sil", [128, 512], F32)
    qst = mk.flex("l0f_qst", [128, 16], F32)
    sqd = mk.flex("l0f_sqd", [128, 1024], BF16)
    onesrow = mk.flex("l0f_ones", [1, 128], F32)
    assert mk.flex_off + SBUF_BASE <= SBUF_LIMIT

    V.memset(onesrow[:, :], 1.0).inc(sV, 1)
    sWSF = [mk.dsem(), mk.dsem()]
    sSCW = [mk.dsem(), mk.dsem()]
    sWGB = [mk.dsem(), mk.dsem()]
    sWUB = [mk.dsem(), mk.dsem()]

    mats = [(wgF_d, wgq_s, 8, 0), (wuF_d, wuq_s, 8, 1), (wdF_d, wdq_s, 8, 2)]
    for (wd, _, nch, mi) in mats:
        V.memset(stat[:, 20 + mi:21 + mi], 0.0).inc(sV, 1)
        for c in range(nch):
            bi = c % 2
            S.wg(buf.get(("wsf", bi)))
            S.dma_start(out=WSF[bi][:, :], in_=wd[:, 4096 * c:4096 * (c + 1)]).inc(sWSF[bi], 16)
            V.wg(sWSF[bi].guard())
            V.reduce_sum(stat[:, 24:25], WSF[bi][:, :], axis=AX.X,
                         apply_absolute_value=True).inc(sV, 1)
            buf[("wsf", bi)] = sV.guard()
            V.drain()
            V.tensor_tensor(stat[:, 20 + mi:21 + mi], stat[:, 20 + mi:21 + mi],
                            stat[:, 24:25], ALU.add)
            V.drain()
        V.drain().inc(sV, 1)
        T.wg(sV.guard())
        T.wg(mk.g_idf)
        T.wg(bank_free[6])
        T.transpose(P[6][0:1, 0:128], stat[:, 20 + mi:21 + mi], identf[:, :]).inc(sT, 1)
        V.wait(sT)
        V.reduce_sum(stat[0:1, 24:25], P[6][0:1, 0:128], axis=AX.X).inc(sV, 1)
        bank_free[6] = sV.guard()
        V.drain()
        V.tensor_scalar(stat[0:1, 24:25], stat[0:1, 24:25], 1.0 / 4194304.0, 1e-5,
                        ALU.mult, ALU.max).inc(sV, 1)
        T.wait(sV)
        T.wg(bank_free[6])
        T.matmul(P[6][:, 0:1], onesrow[0:1, :], stat[0:1, 24:25],
                 start=True, stop=True).inc(sT, 1)
        V.wait(sT)
        V.tensor_copy(qst[:, 2 * mi:2 * mi + 1], P[6][:, 0:1]).inc(sV, 1)
        bank_free[6] = sV.guard()
        V.drain()
        V.reciprocal(qst[:, 2 * mi + 1:2 * mi + 2], qst[:, 2 * mi:2 * mi + 1]).inc(sV, 1)
        V.drain()
    g_tw = sV.guard()

    for (wd, ws, nch, mi) in mats:
        sw = qst[:, 2 * mi + 1:2 * mi + 2]
        for c in range(nch):
            bi = c % 2
            S.wg(buf.get(("wsf", bi)))
            S.dma_start(out=WSF[bi][:, :], in_=wd[:, 4096 * c:4096 * (c + 1)]).inc(sWSF[bi], 16)
            V.wg(sWSF[bi].guard())
            V.wg(buf.get(("wqb", bi)))
            V.tensor_scalar(WSF[bi][:, :], WSF[bi][:, :], sw, MAGIC, ALU.mult, ALU.add)
            V.drain()
            V.tensor_scalar(WSF[bi][:, :], WSF[bi][:, :], MAGIC, -1.0, ALU.subtract, ALU.max)
            V.drain()
            hnd = V.tensor_scalar(WQB[bi][:, :], WSF[bi][:, :], 1.0, None, ALU.min)
            hnd.inc(sV, 1)
            buf[("wsf", bi)] = sV.guard()
            G.wait(sV)
            if mi < 2:
                ws4 = ws.rearrange("p (n k s) -> p n k s", n=8, k=8)
                G.dma_start(out=ws4[:, :, c, :], in_=WQB[bi][:, :]).inc(sSCW[bi], 16)
            else:
                G.dma_start(out=ws[:, 4096 * c:4096 * (c + 1)], in_=WQB[bi][:, :]).inc(sSCW[bi], 16)
            buf[("wqb", bi)] = sSCW[bi].guard()
    g_scr = sSCW[0].guard()
    g_scr2 = sSCW[1].guard()

    for sp in range(8):
        if sp == 0:
            V.wg(g_scr)
            V.wg(g_scr2)
            C.wg(g_scr)
            C.wg(g_scr2)
        for lt2 in range(2):
            lt = 2 * sp + lt2
            src = R[:, 1024 * lt:1024 * (lt + 1)]
            g_st = mk.ln_stats(src, sqd[:, :])
            V.wg(buf.get(("l0f_xf",)))
            g_xf = mk.ln_norm(src, xf[:, :], g_st)
            V.wg(g_xf)
            V.reduce_max(stat[:, 25:26], xf[:, :], axis=AX.X,
                         apply_absolute_value=True).inc(sV, 1)
            V.drain()
            V.tensor_scalar(stat[:, 25:26], stat[:, 25:26], 1e-5, None, ALU.max)
            V.drain()
            V.reciprocal(stat[:, 26:27], stat[:, 25:26])
            V.drain()
            V.tensor_scalar(stat[:, 27:28], stat[:, 26:27], 127.0, None, ALU.mult)
            V.drain()
            V.tensor_scalar(stat[:, 28:29], stat[:, 25:26], 1.0 / 127.0, None, ALU.mult)
            V.drain()
            V.tensor_tensor(qst[:, 8 + lt2:9 + lt2], stat[:, 28:29], qst[:, 0:1], ALU.mult)
            V.tensor_tensor(qst[:, 10 + lt2:11 + lt2], stat[:, 28:29], qst[:, 2:3], ALU.mult)
            V.tensor_copy(qst[:, 14 + lt2:15 + lt2], stat[:, 28:29])
            V.drain()
            V.wg(buf.get(("l0f_xq",)))
            V.tensor_scalar(xf[:, :], xf[:, :], stat[:, 27:28], MAGIC, ALU.mult, ALU.add)
            V.drain()
            V.tensor_scalar(xf[:, :], xf[:, :], MAGIC, -128.0, ALU.subtract, ALU.max)
            V.drain()
            V.tensor_scalar(xq[:, :], xf[:, :], 127.0, None, ALU.min).inc(sV, 1)
            buf[("l0f_xf",)] = sV.guard()
            g_xq = sV.guard()
            mk.tset(xq, lambda i, l_=lt2: XQT[:, 256 * i + 128 * l_:256 * i + 128 * (l_ + 1)],
                    8, g_xq, g_dst_free=buf.get(("l0f_xqt", lt2)))
            buf[("l0f_xq",)] = sT.guard()
        g_xqt = sV.guard()
        for nch in range(8):
            bi = nch % 2
            for (wsrc, wb, tag, sm) in ((wgq_s, WGB, "wgb", sWGB), (wuq_s, WUB, "wub", sWUB)):
                G.wg(buf.get((tag, bi)))
                G.wg(g_scr)
                G.wg(g_scr2)
                G.dma_start(out=wb[bi][:, :],
                            in_=wsrc[:, 4096 * nch:4096 * (nch + 1)]).inc(sm[bi], 16)
            g_w = (sWGB[bi], sWGB[bi].count)
            g_wu = (sWUB[bi], sWUB[bi].count)
            for lt2 in range(2):
                for (wb, pk0) in ((WGB, 0), (WUB, 2)):
                    pk = pk0 + lt2
                    T.wg(g_xqt)
                    T.wg(g_w if pk0 == 0 else g_wu)
                    T.wg(bank_free[pk])
                    for k in range(8):
                        hnd = T.matmul(P[pk][:, :],
                                       XQT[:, 256 * k + 128 * lt2:256 * k + 128 * (lt2 + 1)],
                                       wb[bi][:, 512 * k:512 * (k + 1)],
                                       start=(k == 0), stop=(k == 7))
                    hnd.inc(sT, 1)
                if nch == 7 and lt2 == 1:
                    for l_ in range(2):
                        buf[("l0f_xqt", l_)] = sT.guard()
                buf[("wgb", bi)] = sT.guard()
                buf[("wub", bi)] = sT.guard()
                C.wait(sT)
                C.wg(buf.get(("l0f_sil",)))
                C.activation(sil[:, :], P[0 + lt2][:, :], AF.Silu,
                             scale=qst[:, 8 + lt2:9 + lt2]).inc(sC, 1)
                V.wait(sC)
                V.scalar_tensor_tensor(GS[:, 4096 * lt2 + 512 * nch:4096 * lt2 + 512 * (nch + 1)],
                                       P[2 + lt2][:, :], qst[:, 10 + lt2:11 + lt2],
                                       sil[:, :], ALU.mult, ALU.mult).inc(sV, 1)
                buf[("l0f_sil",)] = sV.guard()
                bank_free[0 + lt2] = sV.guard()
                bank_free[2 + lt2] = sV.guard()
        g_gs = sV.guard()
        for lt2 in range(2):
            gsl = GS[:, 4096 * lt2:4096 * (lt2 + 1)]
            V.wg(g_gs)
            V.reduce_max(stat[:, 25:26], gsl, axis=AX.X, apply_absolute_value=True)
            V.drain()
            V.tensor_scalar(stat[:, 25:26], stat[:, 25:26], 1e-5, None, ALU.max)
            V.drain()
            V.reciprocal(stat[:, 26:27], stat[:, 25:26])
            V.drain()
            V.tensor_scalar(stat[:, 27:28], stat[:, 26:27], 127.0, None, ALU.mult)
            V.drain()
            V.tensor_scalar(stat[:, 28:29], stat[:, 25:26], 1.0 / 127.0, None, ALU.mult)
            V.drain()
            V.tensor_tensor(qst[:, 12 + lt2:13 + lt2], stat[:, 28:29], qst[:, 4:5], ALU.mult)
            V.drain()
            V.tensor_scalar(gsl, gsl, stat[:, 27:28], MAGIC, ALU.mult, ALU.add)
            V.drain()
            V.tensor_scalar(gsl, gsl, MAGIC, -128.0, ALU.subtract, ALU.max)
            V.drain()
            V.wg(buf.get(("l0f_gq",)))
            V.tensor_scalar(gq[:, :], gsl, 127.0, None, ALU.min).inc(sV, 1)
            g_gq = sV.guard()
            mk.tset(gq, lambda i, l_=lt2: GQT[:, 256 * i + 128 * l_:256 * i + 128 * (l_ + 1)],
                    32, g_gq, g_dst_free=buf.get(("l0f_gqt", lt2)))
            buf[("l0f_gq",)] = sT.guard()
        g_gqt = sV.guard()
        for kg in range(8):
            bi = kg % 2
            G.wg(buf.get(("wgb", bi)))
            G.wg(g_scr)
            G.wg(g_scr2)
            G.dma_start(out=WDB[bi][:, :],
                        in_=wdq_s[:, 4096 * kg:4096 * (kg + 1)]).inc(sWGB[bi], 16)
            g_w = sWGB[bi].guard()
            T.wg(g_gqt)
            T.wg(g_w)
            if kg == 0:
                for pk in range(4):
                    T.wg(bank_free[pk])
            for k4 in range(4):
                k = 4 * kg + k4
                for lt2 in range(2):
                    for dch in range(2):
                        hnd = T.matmul(P[2 * lt2 + dch][:, :],
                                       GQT[:, 256 * k + 128 * lt2:256 * k + 128 * (lt2 + 1)],
                                       WDB[bi][:, 1024 * k4 + 512 * dch:1024 * k4 + 512 * (dch + 1)],
                                       start=(k == 0), stop=(k == 31))
            hnd.inc(sT, 1)
            buf[("wgb", bi)] = sT.guard()
        g_dn = sT.guard()
        V.wg(g_dn)
        for lt2 in range(2):
            lt = 2 * sp + lt2
            for dch in range(2):
                cs = 1024 * lt + 512 * dch
                hnd = V.scalar_tensor_tensor(R[:, cs:cs + 512], P[2 * lt2 + dch][:, :],
                                             qst[:, 12 + lt2:13 + lt2], R[:, cs:cs + 512],
                                             ALU.mult, ALU.add)
        hnd.inc(sV, 1)
        for lt2 in range(2):
            buf[("l0f_gqt", lt2)] = sV.guard()
        for pk in range(4):
            bank_free[pk] = sV.guard()



def emit_l1attn(mk, R, P, PB, AR, identb, identf, winT_sl_d, dtwT_sl_d,
                woutT_d, convw_d, convb_d, dtb_d, A_d, Dp_d,
                edge_in, edge_out, car_in, car_out, mh1, mh0):
    nc = mk.nc
    S, G, V, C, T = mk.S, mk.G, mk.V, mk.C, mk.T
    sems = mk.sems
    sS, sG, sT, sV, sC, sCC = (sems[k] for k in ("sS", "sG", "sT", "sV", "sC", "sCC"))
    stat = mk.stat
    bank_free = mk.bank_free
    buf = mk.buf

    XI = mk.at("XI", [128, 24576], BF16, AR)
    SZ = mk.at("SZ", [128, 24576], BF16, AR + 49152)
    WOUT = mk.at("WOUT", [128, 12288], BF16, AR + 49152)

    mk.flex_off = mk.FLEX0
    xn = mk.flex("l1_xn", [128, 1024], BF16)
    xnt = mk.flex("l1_xnt", [128, 4096], BF16)
    wbuf = [mk.flex(f"l1_w{i}", [128, 1536], BF16) for i in range(2)]
    sqd = mk.flex("l1_sqd", [128, 1024], BF16)
    scf0 = mk.flex("l1_s0", [128, 512], F32)
    scf1 = mk.flex("l1_s1", [128, 512], F32)
    scf2 = mk.flex("l1_s2", [128, 512], F32)
    ctmp = mk.flex("l1_ct", [128, 2048], F32)
    p1st = mk.flex("l1_p1", [128, 6144], BF16)
    carry = mk.flex("l1_carry", [128, 12], F32)
    tot = mk.flex("l1_tot", [128, 12], F32)
    haloL = mk.flex("l1_hl", [128, 12], F32)
    haloR = mk.flex("l1_hr", [128, 12], F32)
    ed = mk.flex("l1_ed", [128, 24], F32)
    edT = mk.flex("l1_edT", [24, 128], F32)
    car_sb = mk.flex("l1_carT", [12, 128], F32)
    cw = [mk.flex(f"l1_cw{k}", [128, 12], F32) for k in range(3)]
    cb = mk.flex("l1_cb", [128, 12], F32)
    dtb = mk.flex("l1_dtb", [128, 12], F32)
    Ab = mk.flex("l1_A", [128, 12], F32)
    Dpb = mk.flex("l1_Dp", [128, 12], F32)
    assert mk.flex_off + SBUF_BASE <= SBUF_LIMIT

    sCst1 = mk.dsem()
    sWB1 = [mk.dsem(), mk.dsem()]
    win_bf, dtw_bf = mk.win_bf, mk.dtw_bf
    sGx = mk.dsem()
    sWOg = mk.dsem()
    for k in range(3):
        S.dma_start(out=cw[k][:, :], in_=convw_d[k]).inc(sCst1, 16)
    S.dma_start(out=cb[:, :], in_=convb_d).inc(sCst1, 16)
    S.dma_start(out=dtb[:, :], in_=dtb_d).inc(sCst1, 16)
    S.dma_start(out=Ab[:, :], in_=A_d).inc(sCst1, 16)
    S.dma_start(out=Dpb[:, :], in_=Dp_d).inc(sCst1, 16)
    g_cst = sCst1.guard()

    for lc in range(4):
        for lt2 in range(4):
            lt = 4 * lc + lt2
            src = R[:, 1024 * lt:1024 * (lt + 1)]
            g_st = mk.ln_stats(src, sqd[:, :])
            C.wg(buf.get(("l1_xn",)))
            g_xn = mk.ln_norm(src, xn[:, :], g_st)
            mk.tset(xn, lambda i, l_=lt2: xnt[:, 512 * i + 128 * l_:512 * i + 128 * (l_ + 1)],
                    8, g_xn, g_dst_free=buf.get(("l1_xnt", lt2)))
            buf[("l1_xn",)] = sT.guard()
        g_xnt = sV.guard()
        for et in range(24):
            bi = et % 2
            S.wg(buf.get(("l1_w", bi)))
            if lc == 0 and et < 2:
                S.wg(mk.g_precast)
            S.dma_start(out=wbuf[bi][:, 0:1024], in_=win_bf[et]).inc(sWB1[bi], 16)
            g_w = sWB1[bi].guard()
            pk = et % 6
            T.wg(g_xnt)
            T.wg(g_w)
            T.wg(bank_free[pk])
            for k in range(8):
                hnd = T.matmul(P[pk][:, :], wbuf[bi][:, 128 * k:128 * (k + 1)],
                               xnt[:, 512 * k:512 * (k + 1)],
                               start=(k == 0), stop=(k == 7))
            hnd.inc(sT, 1)
            buf[("l1_w", bi)] = sT.guard()
            if et == 23:
                for l_ in range(4):
                    buf[("l1_xnt", l_)] = sT.guard()
            if et < 12:
                V.wait(sT)
                V.tensor_copy(XI[:, 2048 * et + 512 * lc:2048 * et + 512 * (lc + 1)],
                              P[pk][:, :]).inc(sV, 1)
                bank_free[pk] = sV.guard()
            else:
                C.wait(sT)
                C.activation(SZ[:, 2048 * (et - 12) + 512 * lc:2048 * (et - 12) + 512 * (lc + 1)],
                             P[pk][:, :], AF.Silu).inc(sC, 1)
                bank_free[pk] = sC.guard()
    g_win_v = sV.guard()
    g_win_c = sC.guard()

    V.wg(g_win_v)
    for et in range(12):
        V.tensor_copy(ed[:, et:et + 1], XI[:, 2048 * et:2048 * et + 1])
        V.tensor_copy(ed[:, 12 + et:13 + et], XI[:, 2048 * et + 2047:2048 * et + 2048])
    V.drain().inc(sV, 1)
    g_ed = sV.guard()
    T.wg(g_ed)
    T.wg(bank_free[6])
    T.wg(mk.g_idf)
    T.transpose(P[6][0:24, 0:128], ed[:, :], identf[:, :]).inc(sT, 1)
    V.wait(sT)
    V.tensor_copy(edT[:, :], P[6][0:24, 0:128]).inc(sV, 1)
    bank_free[6] = sV.guard()
    G.wait(sV)
    G.dma_start(out=edge_in[:, :], in_=edT[:, :]).inc(sGx, 16)
    G.wg(sGx.guard())
    G.collective_compute("AllGather", ALU.bypass,
                         replica_groups=[[0, 1], [2, 3], [4, 5], [6, 7]],
                         ins=[edge_in.ap()], outs=[edge_out.ap()]).inc(sCC, 1)
    G.wait(sCC)
    G.dma_start(out=edT[:, :], in_=edge_out[12:36, :]).inc(sGx, 16)
    g_edb = sGx.guard()
    T.wg(g_edb)
    T.wg(bank_free[6])
    T.wg(mk.g_idf)
    T.transpose(P[6][0:128, 0:24], edT[:, :], identf[0:24, 0:24]).inc(sT, 1)
    V.wait(sT)
    V.tensor_scalar(haloL[:, :], P[6][:, 0:12], mh1[:, 0:1], None, ALU.mult)
    V.tensor_scalar(haloR[:, :], P[6][:, 12:24], mh0[:, 0:1], None, ALU.mult).inc(sV, 1)
    bank_free[6] = sV.guard()
    g_halo = sV.guard()

    for et in range(12):
        xi = XI[:, 2048 * et:2048 * (et + 1)]
        V.wg(g_halo)
        V.wg(g_cst)
        V.wg(buf.get(("l1_ct",)))
        V.tensor_scalar(ctmp[:, :], xi, cw[1][:, et:et + 1], None, ALU.mult)
        V.drain()
        V.scalar_tensor_tensor(ctmp[:, 1:2048], xi[:, 0:2047], cw[0][:, et:et + 1],
                               ctmp[:, 1:2048], ALU.mult, ALU.add)
        V.drain()
        V.scalar_tensor_tensor(ctmp[:, 0:1], haloL[:, et:et + 1], cw[0][:, et:et + 1],
                               ctmp[:, 0:1], ALU.mult, ALU.add)
        V.drain()
        V.scalar_tensor_tensor(ctmp[:, 0:2047], xi[:, 1:2048], cw[2][:, et:et + 1],
                               ctmp[:, 0:2047], ALU.mult, ALU.add)
        V.drain()
        V.scalar_tensor_tensor(ctmp[:, 2047:2048], haloR[:, et:et + 1], cw[2][:, et:et + 1],
                               ctmp[:, 2047:2048], ALU.mult, ALU.add).inc(sV, 1)
        C.wait(sV)
        C.activation(xi, ctmp[:, :], AF.Silu, bias=cb[:, et:et + 1]).inc(sC, 1)
        buf[("l1_ct",)] = sC.guard()
    g_conv = sC.guard()

    V.memset(carry[:, :], 0.0).inc(sV, 1)
    for lc in range(4):
        for e2 in range(12):
            bi = e2 % 2
            S.wg(buf.get(("l1_w", bi)))
            if lc == 0 and e2 < 2:
                S.wg(mk.g_precast)
            S.dma_start(out=wbuf[bi][:, :], in_=dtw_bf[e2]).inc(sWB1[bi], 16)
            g_w = sWB1[bi].guard()
            pk = e2 % 4
            T.wg(g_conv)
            T.wg(g_w)
            T.wg(bank_free[pk])
            if lc > 0 or e2 > 0:
                T.wg(buf.get(("l1_p1w",)))
            for k in range(12):
                hnd = T.matmul(P[pk][:, :], wbuf[bi][:, 128 * k:128 * (k + 1)],
                               XI[:, 2048 * k + 512 * lc:2048 * k + 512 * (lc + 1)],
                               start=(k == 0), stop=(k == 11))
            hnd.inc(sT, 1)
            buf[("l1_w", bi)] = sT.guard()
            C.wait(sT)
            C.wg(buf.get(("l1_sc",)))
            C.activation(scf0[:, :], P[pk][:, :], AF.Relu, bias=dtb[:, e2:e2 + 1])
            C.drain()
            C.activation(scf1[:, :], P[pk][:, :], AF.Abs, bias=dtb[:, e2:e2 + 1]).inc(sC, 1)
            bank_free[pk] = sC.guard()
            C.drain()
            C.activation(scf1[:, :], scf1[:, :], AF.Exp, scale=-1.0)
            C.drain()
            C.activation(scf1[:, :], scf1[:, :], AF.Ln, bias=mk.onec[:, 0:1]).inc(sC, 1)
            V.wait(sC)
            V.tensor_tensor(scf0[:, :], scf0[:, :], scf1[:, :], ALU.add).inc(sV, 1)
            C.wait(sV)
            C.activation(scf1[:, :], scf0[:, :], AF.Exp, scale=Ab[:, e2:e2 + 1]).inc(sC, 1)
            V.wait(sC)
            xc = XI[:, 2048 * e2 + 512 * lc:2048 * e2 + 512 * (lc + 1)]
            sz = SZ[:, 2048 * e2 + 512 * lc:2048 * e2 + 512 * (lc + 1)]
            V.tensor_tensor(scf2[:, :], xc, scf0[:, :], ALU.mult)
            V.drain()
            V.tensor_tensor(scf2[:, :], scf2[:, :], scf1[:, :], ALU.mult)
            V.drain()
            V.tensor_tensor_scan(scf0[:, :], scf2[:, :], scf2[:, :],
                                 carry[:, e2:e2 + 1], ALU.add, ALU.bypass)
            V.drain()
            V.tensor_copy(carry[:, e2:e2 + 1], scf0[:, 511:512])
            V.drain()
            V.tensor_tensor(scf0[:, :], scf0[:, :], scf1[:, :], ALU.mult)
            V.drain()
            V.scalar_tensor_tensor(scf0[:, :], xc, Dpb[:, e2:e2 + 1], scf0[:, :],
                                   ALU.mult, ALU.add)
            V.drain()
            V.tensor_tensor(p1st[:, 512 * e2:512 * (e2 + 1)], scf0[:, :], sz, ALU.mult)
            V.drain()
            V.tensor_tensor(sz, scf1[:, :], sz, ALU.mult).inc(sV, 1)
            buf[("l1_sc",)] = sV.guard()
        hnd = None
        for e2 in range(12):
            hnd = V.tensor_copy(XI[:, 2048 * e2 + 512 * lc:2048 * e2 + 512 * (lc + 1)],
                                p1st[:, 512 * e2:512 * (e2 + 1)])
        hnd.inc(sV, 1)
        buf[("l1_p1w",)] = sV.guard()
    V.drain()
    V.tensor_copy(tot[:, :], carry[:, :]).inc(sV, 1)
    g_tot = sV.guard()

    T.wg(g_tot)
    T.wg(bank_free[6])
    T.wg(mk.g_idf)
    T.transpose(P[6][0:12, 0:128], tot[:, :], identf[:, :]).inc(sT, 1)
    V.wait(sT)
    V.tensor_copy(car_sb[:, :], P[6][0:12, 0:128]).inc(sV, 1)
    bank_free[6] = sV.guard()
    G.wait(sV)
    G.dma_start(out=car_in[:, :], in_=car_sb[:, :]).inc(sGx, 16)
    G.wg(sGx.guard())
    G.collective_compute("AllGather", ALU.bypass,
                         replica_groups=[[0, 1], [2, 3], [4, 5], [6, 7]],
                         ins=[car_in.ap()], outs=[car_out.ap()]).inc(sCC, 1)
    G.wait(sCC)
    G.dma_start(out=car_sb[:, :], in_=car_out[0:12, :]).inc(sGx, 16)
    g_carb = sGx.guard()
    T.wg(g_carb)
    T.wg(bank_free[6])
    T.wg(mk.g_idf)
    T.transpose(P[6][0:128, 0:12], car_sb[:, :], identf[0:12, 0:12]).inc(sT, 1)
    V.wait(sT)
    V.tensor_scalar(carry[:, :], P[6][:, 0:12], mh1[:, 0:1], None, ALU.mult).inc(sV, 1)
    bank_free[6] = sV.guard()
    g_carry = sV.guard()

    V.wg(g_carry)
    hnd = None
    for e2 in range(12):
        hnd = V.scalar_tensor_tensor(XI[:, 2048 * e2:2048 * (e2 + 1)],
                                     SZ[:, 2048 * e2:2048 * (e2 + 1)],
                                     carry[:, e2:e2 + 1],
                                     XI[:, 2048 * e2:2048 * (e2 + 1)],
                                     ALU.mult, ALU.add)
    hnd.inc(sV, 1)
    g_y = sV.guard()

    G.wg(g_y)
    G.dma_start(out=WOUT[:, :], in_=woutT_d).inc(sWOg, 16)
    g_wout = sWOg.guard()
    for lt in range(16):
        for n in range(2):
            pk = n + 2 * (lt % 2)
            T.wg(g_y)
            T.wg(g_wout)
            T.wg(bank_free[pk])
            for k in range(12):
                hnd = T.matmul(P[pk][:, :], XI[:, 2048 * k + 128 * lt:2048 * k + 128 * (lt + 1)],
                               WOUT[:, 1024 * k + 512 * n:1024 * k + 512 * (n + 1)],
                               start=(k == 0), stop=(k == 11))
            hnd.inc(sT, 1)
        V.wait(sT)
        for n in range(2):
            pk = n + 2 * (lt % 2)
            cs = 1024 * lt + 512 * n
            hnd = V.tensor_tensor(R[:, cs:cs + 512], P[pk][:, :], R[:, cs:cs + 512], ALU.add)
        hnd.inc(sV, 1)
        for n in range(2):
            bank_free[n + 2 * (lt % 2)] = sV.guard()



def emit_l1ffn(mk, R, P, PB, AR, identb, onesb,
               w1_sl_d, w3_sl_d, w2T_d, routerT_d, ebias_d):
    nc = mk.nc
    S, G, V, C, T = mk.S, mk.G, mk.V, mk.C, mk.T
    sems = mk.sems
    sS, sG, sT, sV, sC = (sems[k] for k in ("sS", "sG", "sT", "sV", "sC"))
    stat = mk.stat
    bank_free = mk.bank_free
    buf = mk.buf

    XT = mk.at("XT", [128, 16384], BF16, AR)
    GT = mk.at("GT", [128, 16384], BF16, AR + 32768)
    W2 = mk.at("W2", [128, 16384], BF16, AR + 65536)

    mk.flex_off = mk.FLEX0
    xn = mk.flex("mo_xn", [128, 1024], BF16)
    sqd = mk.flex("mo_sqd", [128, 1024], BF16)
    rtw = mk.flex("mo_rtw", [128, 64], BF16)
    ebias = mk.flex("mo_eb", [1, 8], BF16)
    lgt = mk.flex("mo_lgt", [128, 8], F32)
    eq = mk.flex("mo_eq", [128, 8], F32)
    expv = mk.flex("mo_expv", [128, 8], F32)
    comb = mk.flex("mo_comb", [128, 16 * 10], F32)
    w1b = [mk.flex(f"mo_w1{i}", [128, 1024], BF16) for i in range(2)]
    w3b = [mk.flex(f"mo_w3{i}", [128, 1024], BF16) for i in range(2)]
    sil = mk.flex("mo_sil", [128, 512], F32)
    assert mk.flex_off + SBUF_BASE <= SBUF_LIMIT

    sWL3 = mk.dsem()
    sW2 = mk.dsem()
    sW13 = [mk.dsem(), mk.dsem()]
    G.dma_start(out=rtw[:, :], in_=routerT_d).inc(sWL3, 16)
    G.dma_start(out=ebias[:, :], in_=ebias_d).inc(sWL3, 16)
    g_rt = sWL3.guard()

    for lt in range(16):
        src = R[:, 1024 * lt:1024 * (lt + 1)]
        g_st = mk.ln_stats(src, sqd[:, :])
        C.wg(buf.get(("mo_xn",)))
        g_xn = mk.ln_norm(src, xn[:, :], g_st)
        g_xtl = mk.tset(xn, lambda i, l_=lt: XT[:, 2048 * i + 128 * l_:2048 * i + 128 * (l_ + 1)],
                        8, g_xn)
        buf[("mo_xn",)] = sT.guard()
        pk = 6
        T.wg(g_xtl)
        T.wg(bank_free[pk])
        T.wg(g_rt)
        for k in range(8):
            T.matmul(P[pk][:, 0:8], XT[:, 2048 * k + 128 * lt:2048 * k + 128 * (lt + 1)],
                     rtw[:, 8 * k:8 * (k + 1)], start=(k == 0), stop=False)
        T.matmul(P[pk][:, 0:8], onesb[0:1, :], ebias[0:1, :],
                 start=False, stop=True).inc(sT, 1)
        V.wait(sT)
        V.tensor_copy(lgt[:, :], P[pk][:, 0:8]).inc(sV, 1)
        bank_free[pk] = sV.guard()
        V.drain()
        V.reduce_max(stat[:, 30:31], lgt[:, :], axis=AX.X)
        V.drain()
        V.tensor_scalar(eq[:, :], lgt[:, :], stat[:, 30:31], None, ALU.is_ge)
        V.drain()
        V.scalar_tensor_tensor(eq[:, :], eq[:, :], NEG, lgt[:, :], ALU.mult, ALU.add)
        V.drain()
        V.reduce_max(stat[:, 31:32], eq[:, :], axis=AX.X)
        V.drain()
        V.tensor_scalar(stat[:, 33:34], stat[:, 30:31], -1.0, None, ALU.mult).inc(sV, 1)
        C.wait(sV)
        C.activation(stat[:, 32:33], stat[:, 31:32], AF.Exp, bias=stat[:, 33:34]).inc(sC, 1)
        V.wait(sC)
        V.tensor_scalar(stat[:, 32:33], stat[:, 32:33], 1.0, None, ALU.add)
        V.drain()
        V.reciprocal(stat[:, 34:35], stat[:, 32:33]).inc(sV, 1)
        C.wait(sV)
        C.activation(expv[:, :], lgt[:, :], AF.Exp, bias=stat[:, 33:34]).inc(sC, 1)
        V.wait(sC)
        V.tensor_scalar(eq[:, :], lgt[:, :], stat[:, 31:32], None, ALU.is_ge)
        V.drain()
        V.tensor_scalar(expv[:, :], expv[:, :], stat[:, 34:35], None, ALU.mult)
        V.drain()
        V.tensor_tensor(comb[:, 10 * lt:10 * lt + 8], expv[:, :], eq[:, :], ALU.mult)
        V.drain()
        V.memset(comb[:, 10 * lt + 8:10 * lt + 10], 1.0).inc(sV, 1)
    g_xt = sV.guard()

    for e in range(NEXP):
        g_w2 = None
        for hh in range(2):
            for ht8 in range(8):
                ht = 8 * hh + ht8
                bi = ht8 % 2
                G.wg(buf.get(("mo_w1", bi)))
                G.dma_start(out=w1b[bi][:, :], in_=w1_sl_d[e, ht]).inc(sW13[bi], 16)
                G.dma_start(out=w3b[bi][:, :], in_=w3_sl_d[e, ht]).inc(sW13[bi], 16)
                g_w = sW13[bi].guard()
                if hh == 0 and ht8 == 0:
                    G.wg(buf.get(("mo_w2",)))
                    G.dma_start(out=W2[:, :], in_=w2T_d[e]).inc(sW2, 16)
                    g_w2 = sW2.guard()
                for lch in range(4):
                    T.wg(g_xt)
                    T.wg(g_w)
                    T.wg(bank_free[lch])
                    for k in range(8):
                        hnd = T.matmul(P[lch][:, :], w1b[bi][:, 128 * k:128 * (k + 1)],
                                       XT[:, 2048 * k + 512 * lch:2048 * k + 512 * (lch + 1)],
                                       start=(k == 0), stop=(k == 7))
                    hnd.inc(sT, 1)
                for lch in range(4):
                    pk3 = 4 + (lch % 2)
                    T.wg(bank_free[pk3])
                    for k in range(8):
                        hnd = T.matmul(P[pk3][:, :], w3b[bi][:, 128 * k:128 * (k + 1)],
                                       XT[:, 2048 * k + 512 * lch:2048 * k + 512 * (lch + 1)],
                                       start=(k == 0), stop=(k == 7))
                    hnd.inc(sT, 1)
                    if lch == 3:
                        buf[("mo_w1", bi)] = sT.guard()
                    C.wait(sT)
                    C.wg(buf.get(("mo_sil",)))
                    C.activation(sil[:, :], P[lch][:, :], AF.Silu).inc(sC, 1)
                    V.wait(sC)
                    V.tensor_tensor(GT[:, 2048 * ht8 + 512 * lch:2048 * ht8 + 512 * (lch + 1)],
                                    P[pk3][:, :], sil[:, :], ALU.mult).inc(sV, 1)
                    buf[("mo_sil",)] = sV.guard()
                    bank_free[lch] = sV.guard()
                    bank_free[pk3] = sV.guard()
            g_gt = sV.guard()
            for lt in range(16):
                pk0 = 2 * (lt % 2)
                for n in range(2):
                    pk = pk0 + n
                    T.wg(g_gt)
                    T.wg(g_w2)
                    T.wg(bank_free[pk])
                    for k in range(8):
                        kk = 8 * hh + k
                        hnd = T.matmul(P[pk][:, :],
                                       GT[:, 2048 * k + 128 * lt:2048 * k + 128 * (lt + 1)],
                                       W2[:, 1024 * kk + 512 * n:1024 * kk + 512 * (n + 1)],
                                       start=(k == 0), stop=(k == 7))
                    hnd.inc(sT, 1)
                if lt == 15:
                    buf[("mo_w2",)] = sT.guard()
                V.wait(sT)
                for n in range(2):
                    cs = 1024 * lt + 512 * n
                    hnd = V.scalar_tensor_tensor(R[:, cs:cs + 512], P[pk0 + n][:, :],
                                                 comb[:, 10 * lt + e:10 * lt + e + 1],
                                                 R[:, cs:cs + 512], ALU.mult, ALU.add)
                hnd.inc(sV, 1)
                for n in range(2):
                    bank_free[pk0 + n] = sV.guard()


_NC_CACHE = {}


def _get_nc(stages=4):
    if stages not in _NC_CACHE:
        _NC_CACHE[stages] = build_program(stages)
    return _NC_CACHE[stages]


def kernel(**inputs):
    per = _host_prep(inputs)
    nc = _get_nc(4)
    res = run_bass_kernel_spmd(nc, per, core_ids=list(range(8)))
    out = np.empty((B, T, D), np.float32)
    for core in range(8):
        b, h = core // 2, core % 2
        out[b, 2048 * h:2048 * h + 2048] = res.results[core]["out"]
    return out
